# version 08
# speedup vs baseline: 1.5574x; 1.0102x over previous
import contextlib
import numpy as np
import ml_dtypes
import concourse.bass as bass
import concourse.mybir as mybir
from concourse.bass_utils import run_bass_kernel_spmd
from concourse.alu_op_type import AluOpType as OP

F32 = mybir.dt.float32
BF16 = mybir.dt.bfloat16
I32 = mybir.dt.int32
AF = mybir.ActivationFunctionType
AX = mybir.AxisListType
BFNP = ml_dtypes.bfloat16

NCORES = 8
T = 1024
D = 4096
NT = T // 128
ND = D // 128

ENGINES = ("tensor", "vector", "scalar", "gpsimd", "sync")


class Sem:
    __slots__ = ("name", "n", "h")

    def __init__(self, name):
        self.name, self.n, self.h = name, 0, None


CHAIN_ENGINES = ("vector", "scalar", "gpsimd")


class Sched:

    def __init__(self):
        self.q = {e: [] for e in ENGINES}
        self.sems = []
        self.pool = []
        self.pool_next = 0
        self.pooling = False
        self.s_fin = None
        self.selfsem = {e: self.sem("chain_" + e) for e in CHAIN_ENGINES}
        self.last = {e: None for e in CHAIN_ENGINES}
        self.lastany = {e: None for e in ENGINES}
        self.barrier_toks = []
        self.need_barrier = set()

    def barrier(self):
        self.barrier_toks = [t for t in self.lastany.values() if t]
        self.need_barrier = set(ENGINES)

    def sem(self, name):
        if self.pooling:
            if self.pool_next < len(self.pool):
                s = self.pool[self.pool_next]
            else:
                s = Sem(f"pool{len(self.pool)}")
                self.sems.append(s)
                self.pool.append(s)
            self.pool_next += 1
            s.n = 0
            return s
        s = Sem(name)
        self.sems.append(s)
        return s

    def phase(self):
        if self.s_fin is None:
            was = self.pooling
            self.pooling = False
            self.s_fin = self.sem("fin")
            self.pooling = was
        for eng in ENGINES:
            self.on(eng, lambda E: E.drain(), inc=self.s_fin, chain=False)
        fin_tok = (self.s_fin, self.s_fin.n)
        last = None
        for s in self.pool[:self.pool_next]:
            last = self.on("gpsimd", lambda E, s=s: E.sem_clear(s.h), wait=[fin_tok, (s, s.n)])
            s.n = 0
        self.barrier_toks = [last] if last else [fin_tok]
        self.need_barrier = set(ENGINES)
        self.pool_next = 0

    def on(self, eng, f, wait=(), inc=None, inck=1, dma=False, chain=True):
        waits = list(wait)
        if eng in self.need_barrier:
            waits += self.barrier_toks
            self.need_barrier.discard(eng)
        if eng in CHAIN_ENGINES and chain and self.last[eng] is not None:
            waits.append(self.last[eng])
        if inc is None and not dma and eng in CHAIN_ENGINES:
            inc, inck = self.selfsem[eng], 1
        if inc is not None:
            inc.n += inck
        byname = {}
        for tok in waits:
            if tok is None:
                continue
            (s, v) = tok
            if s is not None and v > 0:
                byname[id(s)] = (s, max(v, byname.get(id(s), (s, 0))[1]))
        self.q[eng].append((tuple(byname.values()), f, inc, inck))
        tok = (inc, inc.n) if inc is not None else None
        if tok is not None:
            self.lastany[eng] = tok
        if not dma and eng in CHAIN_ENGINES:
            self.last[eng] = tok
        return tok

    def run(self, nc):
        with contextlib.ExitStack() as st:
            for s in self.sems:
                s.h = st.enter_context(nc.semaphore(s.name))
            block = st.enter_context(nc.Block())
            for eng in ENGINES:
                items = self.q[eng]
                if not items:
                    continue

                def emit(E, items=items):
                    for wait, f, inc, k in items:
                        for (s, v) in wait:
                            if v > 0:
                                E.wait_ge(s.h, v)
                        r = f(E)
                        if inc is not None and r is not None:
                            r.then_inc(inc.h, k)
                getattr(block, eng)(emit)


def _enter(st, cm):
    return st.enter_context(cm)


def build_identity(sch, nc, st, tag=""):
    ident = _enter(st, nc.sbuf_tensor("ident" + tag, [128, 128], BF16))
    ones_bf = _enter(st, nc.sbuf_tensor("ones_bf" + tag, [128, 128], BF16))
    ones_f32 = _enter(st, nc.sbuf_tensor("ones_f32" + tag, [128, 1], F32))
    s_const = sch.sem("const" + tag)
    sch.on("gpsimd", lambda E: E.memset(ones_bf[:, :], 1.0))
    sch.on("gpsimd", lambda E: E.memset(ones_f32[:, :], 1.0))
    sch.on("gpsimd", lambda E: E.affine_select(
        ident[:, :], ones_bf[:, :], pattern=[[-1, 128]], compare_op=OP.is_equal,
        fill=0.0, base=0, channel_multiplier=1), inc=s_const)
    return ident, ones_bf, ones_f32, s_const


class LNSet:

    def __init__(self, sch, nc, st, tag, ps_tp):
        self.tag = tag
        self.ps_tp = ps_tp
        self.xb = [_enter(st, nc.sbuf_tensor(f"{tag}_xb{i}", [128, D], F32)) for i in range(2)]
        self.xnb = [_enter(st, nc.sbuf_tensor(f"{tag}_xnb{i}", [128, D], BF16)) for i in range(2)]
        self.stats = [_enter(st, nc.sbuf_tensor(f"{tag}_st{i}", [128, 6 * (D // 512)], F32)) for i in range(2)]
        self.mv = [_enter(st, nc.sbuf_tensor(f"{tag}_mv{i}", [128, 2], F32)) for i in range(2)]
        self.veps = [_enter(st, nc.sbuf_tensor(f"{tag}_ve{i}", [128, 1], F32)) for i in range(2)]
        self.rstd = [_enter(st, nc.sbuf_tensor(f"{tag}_rs{i}", [128, 1], F32)) for i in range(2)]
        self.s_x = [sch.sem(tag + "_x0"), sch.sem(tag + "_x1")]
        self.s_stats = sch.sem(tag + "_stats")
        self.s_rstd = sch.sem(tag + "_rstd")
        self.s_xn = sch.sem(tag + "_xn")
        self.s_fill = sch.sem(tag + "_fill")
        self.s_done = sch.sem(tag + "_done")
        self.n_x = [0, 0]
        self.base = dict(stats=0, rstd=0, xn=0, fill=0, done=0)


def emit_ln_xnT(sch, nc, L, x_hbm, row0, xnT, ident, s_const, x_rows=T, eng_evac="scalar",
                pre_tok=None):
    nt = x_rows // 128
    xb, xnb, stats, mv, veps, rstd = L.xb, L.xnb, L.stats, L.mv, L.veps, L.rstd
    s_x, s_stats, s_rstd, s_xn, s_fill, s_done = L.s_x, L.s_stats, L.s_rstd, L.s_xn, L.s_fill, L.s_done
    b_stats, b_rstd, b_xn, b_fill, b_done = (L.base[k] for k in ("stats", "rstd", "xn", "fill", "done"))

    fills_per_tile = ND // 8
    for i in range(nt):
        b = i % 2
        w = [(s_xn, b_xn + i - 1)] if i >= 2 else ([pre_tok] if pre_tok else [])
        L.n_x[b] += 1
        sch.on("sync", lambda E, i=i, b=b: E.dma_start(out=xb[b][:, :], in_=x_hbm[row0 + i * 128:row0 + (i + 1) * 128, :]),
               wait=w, inc=s_x[b], inck=16)
        for c in range(D // 512):
            wv = [(s_x[b], 16 * L.n_x[b])] if c == 0 else []
            sch.on("vector", lambda E, b=b, c=c: E.bn_stats(stats[b][:, c * 6:(c + 1) * 6], xb[b][:, c * 512:(c + 1) * 512]),
                   wait=wv)
        sch.on("vector", lambda E, b=b: E.bn_aggr(mv[b][:, :], stats[b][:, :]))
        sch.on("vector", lambda E, b=b: E.tensor_scalar(veps[b][:, :], mv[b][:, 1:2], 1e-5, None, OP.add),
               inc=s_stats)
        sch.on("scalar", lambda E, b=b: E.activation(rstd[b][:, :], veps[b][:, :], AF.Ln),
               wait=[(s_stats, b_stats + i + 1)])
        sch.on("scalar", lambda E, b=b: E.activation(rstd[b][:, :], rstd[b][:, :], AF.Exp, scale=-0.5),
               inc=s_rstd)
        w = [(s_rstd, b_rstd + i + 1)] + ([(s_fill, b_fill + (i - 1) * fills_per_tile)] if i >= 2 else [])
        sch.on("vector", lambda E, b=b: E.tensor_scalar(xnb[b][:, :], xb[b][:, :], mv[b][:, 0:1], rstd[b][:, :],
                                                        OP.subtract, OP.mult),
               wait=w, inc=s_xn)
        for g in range(fills_per_tile):
            fillidx = i * fills_per_tile + g
            pbank = L.ps_tp[fillidx % 2]
            w = [(s_const, 1), (s_xn, b_xn + i + 1)]
            if fillidx >= 2:
                w.append((s_done, b_done + fillidx - 1))
            elif pre_tok:
                w.append(pre_tok)
            for m in range(8):
                j = g * 8 + m
                inc = s_fill if m == 7 else None
                sch.on("tensor", lambda E, b=b, j=j, m=m, pbank=pbank: E.transpose(
                    pbank[:, m * 128:(m + 1) * 128], xnb[b][:, j * 128:(j + 1) * 128], ident[:, :]),
                    wait=(w if m == 0 else []), inc=inc)
            xnT_view = xnT[:, :].rearrange("p (s t) -> p s t", t=x_rows)[:, g * 8:(g + 1) * 8, i * 128:(i + 1) * 128]
            pview = pbank[:, :].rearrange("p (s t) -> p s t", t=128)
            if eng_evac == "scalar":
                sch.on("scalar", lambda E, pv=pview, xv=xnT_view: E.copy(xv, pv),
                       wait=[(s_fill, b_fill + fillidx + 1)], inc=s_done)
            else:
                sch.on("vector", lambda E, pv=pview, xv=xnT_view: E.tensor_copy(xv, pv),
                       wait=[(s_fill, b_fill + fillidx + 1)], inc=s_done)
    L.base = dict(stats=s_stats.n, rstd=s_rstd.n, xn=s_xn.n, fill=s_fill.n, done=s_done.n)
    return (s_done, s_done.n)


def dma_bias_cols(sch, nc, st, b_hbm, n, tag, s_bias):
    nb = n // 128
    t = _enter(st, nc.sbuf_tensor(tag, [128, nb], F32))
    for s in range(nb):
        sch.on("sync", lambda E, s=s: E.dma_start(out=t[:, s:s + 1], in_=b_hbm[s * 128:(s + 1) * 128]),
               inc=s_bias, inck=16)
    return t



class Consts:
    pass


_CONSTS_N = [0]


def make_consts(sch, nc, st):
    C = Consts()
    k = _CONSTS_N[0]
    _CONSTS_N[0] += 1
    C.ident, C.ones_bf, C.ones_f32, C.s_const = build_identity(sch, nc, st, tag=f"_{k}")
    C.onesrow_f32 = _enter(st, nc.sbuf_tensor(f"onesrow_f32_{k}", [1, 128], F32))
    sch.on("gpsimd", lambda E: E.memset(C.onesrow_f32[:, :], 1.0), inc=C.s_const)
    ones128f = _enter(st, nc.sbuf_tensor(f"ones128f_c_{k}", [128, 128], F32))
    C.ident_f32 = _enter(st, nc.sbuf_tensor(f"ident_f32_c_{k}", [128, 128], F32))
    sch.on("gpsimd", lambda E: E.memset(ones128f[:, :], 1.0))
    sch.on("gpsimd", lambda E: E.affine_select(
        C.ident_f32[:, :], ones128f[:, :], pattern=[[-1, 128]], compare_op=OP.is_equal,
        fill=0.0, base=0, channel_multiplier=1), inc=C.s_const)
    return C


def emit_A(sch, nc, st, px, C, x, wqT, wkT, wvT, bq, bk, bv, qT_o, kT_o, v_o):
    ident, ones_bf, ones_f32, s_const = C.ident, C.ones_bf, C.ones_f32, C.s_const
    onesrow_f32, ident_f32 = C.onesrow_f32, C.ident_f32
    if True:
        xnT = _enter(st, nc.sbuf_tensor(px + "xnT", [128, ND * T], BF16))
        ps_tp = [_enter(st, nc.psum_tensor(px + f"pstp{i}", [128, 1024], BF16)) for i in range(2)]
        ps = [_enter(st, nc.psum_tensor(px + f"ps{i}", [128, 1024], F32)) for i in range(2)]
        LA = LNSet(sch, nc, st, px + "ln", ps_tp)
        xnT_tok = emit_ln_xnT(sch, nc, LA, x, 0, xnT, ident, s_const)

        s_bias = sch.sem(px + "bias")
        bq_sb = dma_bias_cols(sch, nc, st, bq, 512, px + "bq_sb", s_bias)
        bq_done = s_bias.n
        bk_sb = dma_bias_cols(sch, nc, st, bk, 512, px + "bk_sb", s_bias)
        bk_done = s_bias.n
        bv_row = _enter(st, nc.sbuf_tensor(px + "bvrow", [1, 512], BF16))
        s_biasv = sch.sem(px + "biasv")
        sch.on("gpsimd", lambda E: E.dma_start(out=bv_row[:, :], in_=bv.rearrange("(o n) -> o n", o=1),
                                               max_dma_last_dim=8192),
               inc=s_biasv, inck=16, dma=True)
        bias_total = s_bias.n

        wbuf = [_enter(st, nc.sbuf_tensor(px + f"wb{i}", [128, 4096], BF16)) for i in range(2)]
        s_w = [sch.sem(px + "w0"), sch.sem(px + "w1")]
        s_mmw = sch.sem(px + "mmw")
        s_proj = sch.sem(px + "proj")
        s_evac = sch.sem(px + "evac")
        s_store = sch.sem(px + "store")

        qT_sb = _enter(st, nc.sbuf_tensor(px + "qTsb", [128, 4 * T], BF16))
        kT_sb = _enter(st, nc.sbuf_tensor(px + "kTsb", [128, 4 * T], BF16))
        v_sb = _enter(st, nc.sbuf_tensor(px + "vsb", [128, NT * 512], BF16))

        wave_idx = 0
        dmas_w = 0
        slot_tok = [None, None]
        for (w_hbm, out_sb, bias_sb, wname) in (
                (wqT, qT_sb, bq_sb, "q"), (wkT, kT_sb, bk_sb, "k")):
            for th in range(2):
                banks = [ps[e // 2][:, (e % 2) * 512:(e % 2 + 1) * 512] for e in range(4)]
                for jc in range(4):
                    b = dmas_w % 2
                    w = [slot_tok[b]] if slot_tok[b] else []
                    sch.on("gpsimd", lambda E, jc=jc, b=b, w_hbm=w_hbm: E.dma_start(
                        out=wbuf[b][:, :].rearrange("p (j c) -> p j c", c=512),
                        in_=w_hbm[jc * 1024:(jc + 1) * 1024, :].rearrange("(j p) c -> p j c", p=128),
                        max_dma_last_dim=8192),
                        wait=w, inc=s_w[b], inck=16, dma=True)
                    wtok = (s_w[b], 16 * (dmas_w // 2 + 1))
                    tok = None
                    for jj in range(8):
                        j = jc * 8 + jj
                        w = [wtok]
                        if j == 0:
                            w.append(xnT_tok)
                            if wave_idx >= 1:
                                w.append((s_evac, 4 * wave_idx))
                        for e in range(4):
                            tok = sch.on("tensor", lambda E, j=j, jj=jj, e=e, th=th, b=b, banks=banks: E.matmul(
                                banks[e], wbuf[b][:, jj * 512 + e * 128: jj * 512 + (e + 1) * 128],
                                xnT[:, :].rearrange("p (s t) -> p s t", t=T)[:, j, th * 512:(th + 1) * 512],
                                start=(j == 0), stop=(j == ND - 1)),
                                wait=(w if e == 0 else []), inc=(s_mmw if (e == 3 and jj == 7) else None))
                    slot_tok[b] = tok
                    dmas_w += 1
                for e in range(4):
                    ov = out_sb[:, :].rearrange("p (s t) -> p s t", t=T)[:, e, th * 512:(th + 1) * 512]
                    sch.on("scalar", lambda E, e=e, banks=banks, ov=ov, bias_sb=bias_sb: E.activation(
                        ov, banks[e], AF.Identity, bias=bias_sb[:, e:e + 1]),
                        wait=([(s_mmw, s_mmw.n), (s_bias, bias_total)] if e == 0 else []), inc=s_evac)
                wave_idx += 1
        for wv in range(2):
            banks = [ps[tq // 2][:, (tq % 2) * 512:(tq % 2 + 1) * 512] for tq in range(4)]
            tcs = [wv * 4 + q for q in range(4)]
            for jc in range(4):
                b = dmas_w % 2
                w = [slot_tok[b]] if slot_tok[b] else []
                sch.on("gpsimd", lambda E, jc=jc, b=b: E.dma_start(
                    out=wbuf[b][:, :].rearrange("p (j c) -> p j c", c=512),
                    in_=wvT[jc * 1024:(jc + 1) * 1024, :].rearrange("(j p) c -> p j c", p=128),
                    max_dma_last_dim=8192),
                    wait=w, inc=s_w[b], inck=16, dma=True)
                wtok = (s_w[b], 16 * (dmas_w // 2 + 1))
                tok = None
                for jj in range(8):
                    j = jc * 8 + jj
                    w = [wtok]
                    if j == 0:
                        w.append((s_evac, 4 * wave_idx))
                    for q in range(4):
                        tc = tcs[q]
                        tok = sch.on("tensor", lambda E, j=j, jj=jj, q=q, tc=tc, b=b, banks=banks: E.matmul(
                            banks[q], xnT[:, :].rearrange("p (s t) -> p s t", t=T)[:, j, tc * 128:(tc + 1) * 128],
                            wbuf[b][:, jj * 512:(jj + 1) * 512], start=(j == 0), stop=False),
                            wait=(w if q == 0 else []), inc=(s_mmw if (q == 3 and jj == 7) else None))
                slot_tok[b] = tok
                dmas_w += 1
            for q in range(4):
                sch.on("tensor", lambda E, q=q, banks=banks: E.matmul(
                    banks[q], ones_bf[0:1, :], bv_row[:, :], start=False, stop=True),
                    wait=([(s_biasv, 16)] if q == 0 else []), inc=(s_mmw if q == 3 else None))
            for q in range(4):
                tc = tcs[q]
                sch.on("scalar", lambda E, q=q, tc=tc, banks=banks: E.activation(
                    v_sb[:, tc * 512:(tc + 1) * 512], banks[q], AF.Copy),
                    wait=([(s_mmw, s_mmw.n)] if q == 0 else []), inc=s_evac)
            wave_idx += 1

        nstores = 0
        for e in range(4):
            sch.on("sync", lambda E, e=e: E.dma_start(
                out=qT_o[e * 128:(e + 1) * 128, :], in_=qT_sb[:, e * T:(e + 1) * T]),
                wait=[(s_evac, 8)], inc=s_store, inck=16)
            nstores += 1
        for e in range(4):
            sch.on("sync", lambda E, e=e: E.dma_start(
                out=kT_o[e * 128:(e + 1) * 128, :], in_=kT_sb[:, e * T:(e + 1) * T]),
                wait=[(s_evac, 16)], inc=s_store, inck=16)
            nstores += 1
        for tc in range(NT):
            sch.on("sync", lambda E, tc=tc: E.dma_start(
                out=v_o[tc * 128:(tc + 1) * 128, :], in_=v_sb[:, tc * 512:(tc + 1) * 512]),
                wait=[(s_evac, 24)], inc=s_store, inck=16)
            nstores += 1
        sch.on("sync", lambda E: E.nop(), wait=[(s_store, 16 * nstores)])
    return ps, (s_evac, s_evac.n)


def build_A():
    nc = bass.Bass("TRN2", target_bir_lowering=False, debug=False)
    x = nc.dram_tensor("x", [T, D], F32, kind="ExternalInput").ap()
    wqT = nc.dram_tensor("wqT", [D, 512], F32, kind="ExternalInput").ap()
    wkT = nc.dram_tensor("wkT", [D, 512], F32, kind="ExternalInput").ap()
    wvT = nc.dram_tensor("wvT", [D, 512], F32, kind="ExternalInput").ap()
    bq = nc.dram_tensor("bq", [512], F32, kind="ExternalInput").ap()
    bk = nc.dram_tensor("bk", [512], F32, kind="ExternalInput").ap()
    bv = nc.dram_tensor("bv", [512], F32, kind="ExternalInput").ap()
    qT_o = nc.dram_tensor("qT", [512, T], BF16, kind="ExternalOutput").ap()
    kT_o = nc.dram_tensor("kT", [512, T], BF16, kind="ExternalOutput").ap()
    v_o = nc.dram_tensor("v", [T, 512], BF16, kind="ExternalOutput").ap()
    sch = Sched()
    with contextlib.ExitStack() as st:
        C = make_consts(sch, nc, st)
        emit_A(sch, nc, st, "a_", C, x, wqT, wkT, wvT, bq, bk, bv, qT_o, kT_o, v_o)
        sch.run(nc)
    return nc




def emit_B(sch, nc, st, px, C, qT, kTf, vf, x, woT, bo, x1, gathered=False, kv_tok=None):
    TK = 4 * T
    NKT = TK // 128
    ident, ones_bf, ones_f32, s_const = C.ident, C.ones_bf, C.ones_f32, C.s_const
    onesrow_f32, ident_f32 = C.onesrow_f32, C.ident_f32
    if True:
        qT_sb = _enter(st, nc.sbuf_tensor(px + "qTsb", [128, 4 * T], BF16))
        kT_sb = _enter(st, nc.sbuf_tensor(px + "kTsb", [128, 4 * TK], BF16))
        v_sb = _enter(st, nc.sbuf_tensor(px + "vsb", [128, NKT * 512], BF16))
        ctxT_sb = _enter(st, nc.sbuf_tensor(px + "ctxTsb", [128, 4 * T], BF16))
        den_acc = _enter(st, nc.sbuf_tensor(px + "den", [128, T], F32))
        den_acc2 = _enter(st, nc.sbuf_tensor(px + "den2", [128, T], F32))
        expT = [_enter(st, nc.sbuf_tensor(px + f"expT{i}", [128, T], BF16)) for i in range(2)]
        fmx = _enter(st, nc.sbuf_tensor(px + "fmx", [128, 1], F32))
        negm = _enter(st, nc.sbuf_tensor(px + "negm", [1, 1], F32))
        mbias = _enter(st, nc.sbuf_tensor(px + "mbias", [128, 1], F32))
        rden = _enter(st, nc.sbuf_tensor(px + "rden", [1, T], F32))
        rden2 = _enter(st, nc.sbuf_tensor(px + "rden2", [1, T], F32))
        dbc_sb = _enter(st, nc.sbuf_tensor(px + "dbc", [128, T], F32))
        bo_row = _enter(st, nc.sbuf_tensor(px + "borow", [1, D], BF16))

        ps_s = _enter(st, nc.psum_tensor(px + "ps_s", [128, 1024], F32))
        ps_c = _enter(st, nc.psum_tensor(px + "ps_c", [128, 1024], F32))
        ps_m = _enter(st, nc.psum_tensor(px + "ps_m", [128, 1024], F32))
        ps_o = _enter(st, nc.psum_tensor(px + "ps_o", [128, 1024], F32))
        pbanks = [ps_s, ps_c, ps_m, ps_o]

        s_ld = sch.sem(px + "ld")
        nld = 0
        for e in range(4):
            sch.on("sync", lambda E, e=e: E.dma_start(out=qT_sb[:, e * T:(e + 1) * T], in_=qT[e * 128:(e + 1) * 128, :]),
                   inc=s_ld, inck=16); nld += 1
        if not gathered:
            for e in range(4):
                sch.on("sync", lambda E, e=e: E.dma_start(out=kT_sb[:, e * TK:(e + 1) * TK], in_=kTf[e * 128:(e + 1) * 128, :]),
                       inc=s_ld, inck=16); nld += 1
        else:
            if kv_tok:
                pass
            for e in range(4):
                for r in range(4):
                    sch.on("sync", lambda E, e=e, r=r: E.dma_start(
                        out=kT_sb[:, e * TK + r * 1024: e * TK + (r + 1) * 1024],
                        in_=kTf[r * 512 + e * 128: r * 512 + (e + 1) * 128, :]),
                        wait=(list(kv_tok) if (kv_tok and e == 0 and r == 0) else []),
                        inc=s_ld, inck=16); nld += 1
        for s_ in range(NKT):
            sch.on("sync", lambda E, s_=s_: E.dma_start(out=v_sb[:, s_ * 512:(s_ + 1) * 512], in_=vf[s_ * 128:(s_ + 1) * 128, :]),
                   wait=(list(kv_tok) if (kv_tok and s_ == 0) else []),
                   inc=s_ld, inck=16); nld += 1
        s_bor = sch.sem(px + "bor")
        sch.on("gpsimd", lambda E: E.dma_start(out=bo_row[:, :], in_=bo.rearrange("(o n) -> o n", o=1),
                                               max_dma_last_dim=8192), inc=s_bor, inck=16, dma=True)
        ld_tot = 16 * nld


        s_sc = sch.sem(px + "sc"); s_pv = sch.sem(px + "pv"); s_mb = sch.sem(px + "mbm")
        s_dnm = sch.sem(px + "dnm"); s_bcm = sch.sem(px + "bcm"); s_ev = sch.sem(px + "ev")
        exp_tok = [None, None]
        denpv_tok = [None, None]
        den_tok = [None, None]
        dnf_tok = [None]
        evac_tok = None
        mcopy_tok = None
        dbc_tok = None
        sc_toks = {}

        def b_scores(kt, h, base, strip):
            b = kt % 2
            spair = ps_s if b == 0 else ps_o
            w = []
            if kt == 0:
                w.append((s_ld, ld_tot))
                w.append((s_const, s_const.n))
            if exp_tok[b] is not None:
                w.append(exp_tok[b])
            w = [t for t in w if t]
            tok = None
            for half in range(2):
                tok = sch.on("tensor", lambda E, h=h, kt=kt, half=half, base=base, strip=strip, spair=spair: E.matmul(
                    spair[:, half * 512:(half + 1) * 512],
                    kT_sb[base:base + 64, strip * TK + kt * 128: strip * TK + (kt + 1) * 128],
                    qT_sb[base:base + 64, strip * T + half * 512: strip * T + (half + 1) * 512],
                    start=True, stop=True),
                    wait=(w if half == 0 else []), inc=(s_sc if half == 1 else None))
            sc_toks[kt] = tok

        def b_tail(kt, h, base):
            b = kt % 2
            spair = ps_s if b == 0 else ps_o
            w = [sc_toks[kt], mcopy_tok]
            if denpv_tok[b] is not None:
                w.append(denpv_tok[b])
            if den_tok[b] is not None:
                w.append(den_tok[b])
            exp_tok[b] = sch.on("scalar", lambda E, b=b, spair=spair: E.activation(
                expT[b][:, :], spair[:, :], AF.Exp, bias=mbias[:, 0:1]), wait=w)
            deng = "vector" if b == 0 else "gpsimd"
            dacc = den_acc if b == 0 else den_acc2
            if kt <= 1:
                wd_ = [exp_tok[b]] + ([dnf_tok[0]] if (b == 1 and dnf_tok[0]) else [])
                den_tok[b] = sch.on(deng, lambda E, b=b, dacc=dacc: E.tensor_copy(dacc[:, :], expT[b][:, :]),
                                    wait=wd_)
            else:
                den_tok[b] = sch.on(deng, lambda E, b=b, dacc=dacc: E.tensor_tensor(
                    dacc[:, :], dacc[:, :], expT[b][:, :], OP.add), wait=[exp_tok[b]])
            w = [exp_tok[b]]
            if kt == 0 and h > 0:
                w.append(evac_tok)
            for half in range(2):
                denpv_tok[b] = sch.on("tensor", lambda E, h=h, kt=kt, half=half, base=base, b=b: E.matmul(
                    ps_c[base:base + 64, half * 512:(half + 1) * 512],
                    v_sb[:, kt * 512 + h * 64: kt * 512 + (h + 1) * 64],
                    expT[b][:, half * 512:(half + 1) * 512],
                    start=(kt == 0), stop=(kt == NKT - 1)),
                    wait=(w if half == 0 else []), inc=(s_pv if half == 1 else None))

        for h in range(8):
            base = 64 * (h % 2)
            strip = h // 2
            for kt in range(NKT):
                b_scores(kt, h, base, strip)
                if kt == 0:
                    t0 = sch.on("vector", lambda E: E.tensor_reduce(fmx[:, :], ps_s[:, :], AX.X, OP.max),
                                wait=[sc_toks[0]])
                    w2 = [t0, (s_const, s_const.n)]
                    if mcopy_tok is not None:
                        w2.append(mcopy_tok)
                    if dbc_tok is not None:
                        w2.append(dbc_tok)
                    tp_tok = sch.on("tensor", lambda E: E.transpose(
                        ps_m[0:1, 0:128], fmx[:, 0:1], ident_f32[:, :]), wait=w2, inc=s_mb)
                    vr_tok = sch.on("vector", lambda E: E.tensor_reduce(negm[:, :], ps_m[0:1, 0:128], AX.X, OP.max),
                                    wait=[tp_tok])
                    vm_tok = sch.on("vector", lambda E: E.tensor_scalar(negm[:, :], negm[:, :], -1.0, -10.0, OP.mult, OP.add))
                    mb_tok = sch.on("tensor", lambda E: E.matmul(
                        ps_m[:, 0:1], onesrow_f32[:, :], negm[:, :], start=True, stop=True),
                        wait=[vm_tok], inc=s_mb)
                    mcopy_tok = sch.on("scalar", lambda E: E.copy(mbias[:, :], ps_m[:, 0:1]), wait=[mb_tok])
                if kt >= 1:
                    b_tail(kt - 1, h, base)
            b_tail(NKT - 1, h, base)
            w = [den_tok[0], den_tok[1], mcopy_tok]
            dn_tok = None
            for half in range(2):
                for di, dacc in enumerate((den_acc, den_acc2)):
                    dn_tok = sch.on("tensor", lambda E, half=half, dacc=dacc, di=di: E.matmul(
                        ps_m[0:1, half * 512:(half + 1) * 512], ones_f32[:, :],
                        dacc[:, half * 512:(half + 1) * 512], start=(di == 0), stop=(di == 1)),
                        wait=(w if (half == 0 and di == 0) else []),
                        inc=(s_dnm if (half == 1 and di == 1) else None))
            dnf_tok[0] = dn_tok
            r1 = sch.on("vector", lambda E: E.reciprocal(rden[:, :], ps_m[0:1, :]), wait=[dn_tok])
            r2 = sch.on("vector", lambda E: E.tensor_scalar(rden2[:, :], rden[:, :], 0.044194173824159216, None, OP.mult))
            w = [r2, (s_const, s_const.n)]
            bc_tok = None
            for half in range(2):
                bc_tok = sch.on("tensor", lambda E, half=half: E.matmul(
                    ps_m[:, half * 512:(half + 1) * 512], onesrow_f32[:, :],
                    rden2[:, half * 512:(half + 1) * 512], start=True, stop=True),
                    wait=(w if half == 0 else []), inc=(s_bcm if half == 1 else None))
            dbc_tok = sch.on("scalar", lambda E: E.copy(dbc_sb[:, :], ps_m[:, :]), wait=[bc_tok])
            evac_tok = sch.on("vector", lambda E, h=h, base=base, strip=strip: E.tensor_tensor(
                ctxT_sb[base:base + 64, strip * T:(strip + 1) * T],
                ps_c[base:base + 64, :], dbc_sb[base:base + 64, :], OP.mult),
                wait=[denpv_tok[(NKT - 1) % 2], dbc_tok], inc=s_ev)

        wo_buf = [_enter(st, nc.sbuf_tensor(px + f"wob{i}", [128, 2048], BF16)) for i in range(2)]
        x_buf = [_enter(st, nc.sbuf_tensor(px + "xb0", [128, 8 * 512], F32))] * 2
        s_wo = [sch.sem(px + "wo0"), sch.sem(px + "wo1")]
        s_xi = [sch.sem(px + "xi0"), sch.sem(px + "xi1")]
        s_st = [sch.sem(px + "st0"), sch.sem(px + "st1")]
        nst_slot = [0, 0]
        wo_slot_tok = [None, None]
        x_slot_tok = [None, None]
        s_wom = sch.sem(px + "wom"); s_mme = sch.sem(px + "mme")
        bank_free = [evac_tok] * 8
        allfree = [evac_tok, dbc_tok, mcopy_tok, exp_tok[0], exp_tok[1]]
        ndma_wo = 0
        nst = 0
        bank = lambda tc: pbanks[tc // 2][:, (tc % 2) * 512:(tc % 2 + 1) * 512]
        evl_tok = {}
        for dc in range(8):
            xb = 0
            w = [x_slot_tok[xb]] if x_slot_tok[xb] else []
            sch.on("sync", lambda E, dc=dc, xb=xb: E.dma_start(
                out=x_buf[xb][:, :].rearrange("p (n c) -> p n c", c=512),
                in_=x[:, dc * 512:(dc + 1) * 512].rearrange("(n p) c -> p n c", p=128)),
                wait=w, inc=s_xi[xb], inck=16)
            xin_tok = (s_xi[xb], 16 * (dc + 1))
            wb = ndma_wo % 2
            w = [wo_slot_tok[wb]] if wo_slot_tok[wb] else []
            sch.on("gpsimd", lambda E, dc=dc, wb=wb: E.dma_start(
                out=wo_buf[wb][:, :].rearrange("p (s c) -> p s c", c=512),
                in_=woT[:, dc * 512:(dc + 1) * 512].rearrange("(s p) c -> p s c", p=128),
                max_dma_last_dim=8192), wait=w, inc=s_wo[wb], inck=16, dma=True)
            wtok = (s_wo[wb], 16 * (ndma_wo // 2 + 1))
            for s_ in range(4):
                for tc in range(8):
                    w2 = []
                    if tc == 0:
                        w2 = [wtok] + ([t for t in allfree if t] if (dc == 0 and s_ == 0) else [])
                        if s_ == 0 and dc > 0:
                            w2 += [evl_tok[dc - 1]]
                    last = sch.on("tensor", lambda E, s_=s_, tc=tc, wb=wb: E.matmul(
                        bank(tc), ctxT_sb[:, s_ * T + tc * 128: s_ * T + (tc + 1) * 128],
                        wo_buf[wb][:, s_ * 512:(s_ + 1) * 512], start=(s_ == 0), stop=False),
                        wait=w2, inc=(s_wom if tc == 7 else None))
                    if tc == 7:
                        wo_slot_tok[wb] = last
            ndma_wo += 1
            for tc in range(8):
                tk_ = sch.on("tensor", lambda E, tc=tc, dc=dc: E.matmul(
                    bank(tc), ones_bf[0:1, :], bo_row[:, dc * 512:(dc + 1) * 512],
                    start=False, stop=True),
                    wait=([(s_bor, 16)] if tc == 0 else []), inc=(s_mme if tc == 7 else None))
                if tc == 7:
                    mmend_tok = tk_
            ev_last = None
            for tc in range(8):
                ev_last = sch.on("vector", lambda E, tc=tc, xb=xb: E.tensor_tensor(
                    x_buf[xb][:, tc * 512:(tc + 1) * 512], bank(tc),
                    x_buf[xb][:, tc * 512:(tc + 1) * 512], OP.add),
                    wait=([mmend_tok, xin_tok] if tc == 0 else []))
                sch.on("sync", lambda E, tc=tc, dc=dc, xb=xb: E.dma_start(
                    out=x1[tc * 128:(tc + 1) * 128, dc * 512:(dc + 1) * 512],
                    in_=x_buf[xb][:, tc * 512:(tc + 1) * 512]),
                    wait=[ev_last], inc=s_st[xb], inck=16)
                nst_slot[xb] += 1
            evl_tok[dc] = ev_last
            x_slot_tok[xb] = (s_st[xb], 16 * nst_slot[xb])
        sch.on("sync", lambda E: E.nop(), wait=[(s_st[0], 16 * nst_slot[0]), (s_st[1], 16 * nst_slot[1])])


def build_B():
    nc = bass.Bass("TRN2", target_bir_lowering=False, debug=False)
    TK = 4 * T
    NKT = TK // 128
    qT = nc.dram_tensor("qT", [512, T], BF16, kind="ExternalInput").ap()
    kTf = nc.dram_tensor("kTf", [512, TK], BF16, kind="ExternalInput").ap()
    vf = nc.dram_tensor("vf", [TK, 512], BF16, kind="ExternalInput").ap()
    x = nc.dram_tensor("x", [T, D], F32, kind="ExternalInput").ap()
    woT = nc.dram_tensor("woT", [512, D], F32, kind="ExternalInput").ap()
    bo = nc.dram_tensor("bo", [D], F32, kind="ExternalInput").ap()
    x1 = nc.dram_tensor("x1", [T, D], F32, kind="ExternalOutput").ap()
    sch = Sched()
    with contextlib.ExitStack() as st:
        C = make_consts(sch, nc, st)
        emit_B(sch, nc, st, "b_", C, qT, kTf, vf, x, woT, bo, x1)
        sch.run(nc)
    return nc



def emit_C(sch, nc, st, px, C, gq, gs, gz, uq, us, uz, dq, ds, dz, wg, wu, wd, tiled=False,
           ps_ext=None, ps_guard=None):
    I = 11008
    CW = 2048
    ident, ones_bf, ones_f32, s_const = C.ident, C.ones_bf, C.ones_f32, C.s_const
    onesrow_f32, ident_f32 = C.onesrow_f32, C.ident_f32
    if True:
        onesrow_f32 = _enter(st, nc.sbuf_tensor(px + "onesrow_f32", [1, 128], F32))
        s_const = sch.sem(px + "constc")
        sch.on("gpsimd", lambda E: E.memset(onesrow_f32[:, :], 1.0), inc=s_const)
        s_row = _enter(st, nc.sbuf_tensor(px + "s_row", [1, CW], F32))
        z_row = _enter(st, nc.sbuf_tensor(px + "z_row", [1, CW], F32))
        qb = [_enter(st, nc.sbuf_tensor(px + f"qb{i}", [128, CW], BF16)) for i in range(2)]
        t1 = _enter(st, nc.sbuf_tensor(px + "t1", [128, CW], F32))
        wout = [_enter(st, nc.sbuf_tensor(px + f"wout{i}", [128, CW], BF16)) for i in range(2)]
        if ps_ext is None:
            psa = _enter(st, nc.psum_tensor(px + "ps_a", [128, 1024], F32))
            psb = _enter(st, nc.psum_tensor(px + "ps_b", [128, 1024], F32))
        else:
            psa, psb = ps_ext
        s_banks = [psa[:, 0:512], psa[:, 512:1024]]
        z_banks = [psb[:, 0:512], psb[:, 512:1024]]
        s_sz = sch.sem(px + "sz")
        s_q = sch.sem(px + "q")
        s_wst = sch.sem(px + "wst")
        s_bc = sch.sem(px + "bc")
        n_sz = [0]
        n_q = [0]
        n_st = [0]
        state = dict(chunk=0, nslg=0)
        last_v1 = [None, None]
        last_m = [None, None, None]
        last_w = [None, None]
        bank_tok = [[None, None], [None, None]]
        guard_used = [False, False]

        def one_chunk(q_h, s_h, z_h, w_h, g, c0, cw):
            cb = state['chunk'] % 2
            state['chunk'] += 1
            w = [last_m[2]] if last_m[2] else []
            sch.on("sync", lambda E, cw=cw: E.dma_start(out=s_row[:, 0:cw], in_=s_h[g, c0:c0 + cw].rearrange("(o n) -> o n", o=1)),
                   wait=w, inc=s_sz, inck=16); n_sz[0] += 1
            sch.on("sync", lambda E, cw=cw: E.dma_start(out=z_row[:, 0:cw], in_=z_h[g, c0:c0 + cw].rearrange("(o n) -> o n", o=1)),
                   inc=s_sz, inck=16); n_sz[0] += 1
            sztok = (s_sz, 16 * n_sz[0])
            qw = [last_v1[cb]] if last_v1[cb] else []
            qeng = "gpsimd"
            sch.on(qeng, lambda E, cb=cb, cw=cw: E.dma_start(out=qb[cb][:, 0:cw], in_=q_h[g * 128:(g + 1) * 128, c0:c0 + cw],
                                                            max_dma_last_dim=8192),
                   wait=qw, inc=s_q, inck=16, dma=True); n_q[0] += 1
            qtok = (s_q, 16 * n_q[0])
            v1 = sch.on("vector", lambda E, cb=cb, cw=cw: E.tensor_scalar(t1[:, 0:cw], qb[cb][:, 0:cw], -8.0, None, OP.add),
                        wait=[qtok])
            last_v1[cb] = v1
            vtok = v1
            nsl = (cw + 511) // 512
            mlast = None
            for sl in range(nsl):
                slot = state['nslg'] % 2
                state['nslg'] += 1
                c = sl * 512
                sw = min(512, cw - c)
                wbs = [sztok, (s_const, 1)] + ([bank_tok[0][slot]] if bank_tok[0][slot] else [])
                if ps_guard is not None and not guard_used[slot]:
                    guard_used[slot] = True
                    wbs.append(ps_guard)
                m1 = sch.on("tensor", lambda E, c=c, slot=slot, sw=sw: E.matmul(
                    s_banks[slot][:, 0:sw], onesrow_f32[:, :], s_row[:, c:c + sw], start=True, stop=True),
                    wait=wbs, inc=s_bc)
                wbz = [bank_tok[1][slot]] if bank_tok[1][slot] else []
                m2 = sch.on("tensor", lambda E, c=c, slot=slot, sw=sw: E.matmul(
                    z_banks[slot][:, 0:sw], onesrow_f32[:, :], z_row[:, c:c + sw], start=True, stop=True),
                    wait=wbz, inc=s_bc)
                mlast = m2
                v2 = sch.on("vector", lambda E, cb=cb, c=c, slot=slot, sw=sw: E.tensor_tensor(
                    t1[:, c:c + sw], t1[:, c:c + sw], s_banks[slot][:, 0:sw], OP.mult), wait=[m1])
                bank_tok[0][slot] = v2
                wv = [m2] + ([last_w[cb]] if (sl == 0 and last_w[cb]) else [])
                v3 = sch.on("vector", lambda E, cb=cb, c=c, slot=slot, sw=sw: E.tensor_tensor(
                    wout[cb][:, c:c + sw], t1[:, c:c + sw], z_banks[slot][:, 0:sw], OP.add), wait=wv)
                bank_tok[1][slot] = v3
                vtok = v3
            last_m[2] = mlast
            if tiled and w_h is not wd:
                it0 = c0 // 128
                nit = cw // 128
                ov = w_h[it0:it0 + nit, :, g * 128:(g + 1) * 128].rearrange("i p f -> p i f")
                iv = wout[cb][:, 0:cw].rearrange("p (i f) -> p i f", f=128)
                sch.on("scalar", lambda E, ov=ov, iv=iv: E.dma_start(out=ov, in_=iv),
                       wait=[vtok], inc=s_wst, inck=16, dma=True); n_st[0] += 1
            else:
                sch.on("scalar", lambda E, cb=cb, cw=cw: E.dma_start(out=w_h[g * 128:(g + 1) * 128, c0:c0 + cw], in_=wout[cb][:, 0:cw]),
                       wait=[vtok], inc=s_wst, inck=16, dma=True); n_st[0] += 1
            last_w[cb] = (s_wst, 16 * n_st[0])

        for (q_h, s_h, z_h, w_h, ng, width) in ((gq, gs, gz, wg, 4, I), (uq, us, uz, wu, 4, I), (dq, ds, dz, wd, 11, D)):
            for g in range(ng):
                c0 = 0
                while c0 < width:
                    cw = min(CW, width - c0)
                    one_chunk(q_h, s_h, z_h, w_h, g, c0, cw)
                    c0 += cw
        sch.on("sync", lambda E: E.nop(), wait=[(s_wst, 16 * n_st[0])])
    return (s_wst, 16 * n_st[0])


def build_C():
    nc = bass.Bass("TRN2", target_bir_lowering=False, debug=False)
    I = 11008
    gq = nc.dram_tensor("gq", [512, I], I32, kind="ExternalInput").ap()
    gs = nc.dram_tensor("gs", [4, I], F32, kind="ExternalInput").ap()
    gz = nc.dram_tensor("gz", [4, I], F32, kind="ExternalInput").ap()
    uq = nc.dram_tensor("uq", [512, I], I32, kind="ExternalInput").ap()
    us = nc.dram_tensor("us", [4, I], F32, kind="ExternalInput").ap()
    uz = nc.dram_tensor("uz", [4, I], F32, kind="ExternalInput").ap()
    dq = nc.dram_tensor("dq", [1408, D], I32, kind="ExternalInput").ap()
    ds = nc.dram_tensor("ds", [11, D], F32, kind="ExternalInput").ap()
    dz = nc.dram_tensor("dz", [11, D], F32, kind="ExternalInput").ap()
    wg = nc.dram_tensor("wg", [512, I], BF16, kind="ExternalOutput").ap()
    wu = nc.dram_tensor("wu", [512, I], BF16, kind="ExternalOutput").ap()
    wd = nc.dram_tensor("wd", [1408, D], BF16, kind="ExternalOutput").ap()
    sch = Sched()
    with contextlib.ExitStack() as st:
        C = make_consts(sch, nc, st)
        emit_C(sch, nc, st, "c_", C, gq, gs, gz, uq, us, uz, dq, ds, dz, wg, wu, wd)
        sch.run(nc)
    return nc



def emit_D(sch, nc, st, px, C, x1, wg_h, wu_h, wd, x2, gathered=False, w_tok=None):
    I = 11008
    NI = I // 128
    TH = 512
    ident, ones_bf, ones_f32, s_const = C.ident, C.ones_bf, C.ones_f32, C.s_const
    onesrow_f32, ident_f32 = C.onesrow_f32, C.ident_f32
    if True:
        ps_tp = [_enter(st, nc.psum_tensor(px + f"pstp{i}", [128, 1024], BF16)) for i in range(2)]
        psA = _enter(st, nc.psum_tensor(px + "psA", [128, 1024], F32))
        psB = _enter(st, nc.psum_tensor(px + "psB", [128, 1024], F32))
        xnT = _enter(st, nc.sbuf_tensor(px + "xnT", [128, ND * TH], BF16))
        hT = _enter(st, nc.sbuf_tensor(px + "hT", [128, NI * TH], BF16))
        L = LNSet(sch, nc, st, px + "ln", ps_tp)
        wgbuf = [_enter(st, nc.sbuf_tensor(px + f"wgb{i}", [128, 2048], BF16)) for i in range(2)]
        wubuf = [_enter(st, nc.sbuf_tensor(px + f"wub{i}", [128, 2048], BF16)) for i in range(2)]
        wdbuf = [_enter(st, nc.sbuf_tensor(px + f"wdb{i}", [128, 512], BF16)) for i in range(3)]
        tmp = [_enter(st, nc.sbuf_tensor(px + f"tmp{i}", [128, 512], F32)) for i in range(2)]
        xdb = [_enter(st, nc.sbuf_tensor(px + f"xdb{i}", [128, 2 * 512], F32)) for i in range(4)]
        s_wg = [sch.sem(px + "wg0"), sch.sem(px + "wg1")]
        s_wu = [sch.sem(px + "wu0"), sch.sem(px + "wu1")]
        s_wd = [sch.sem(px + f"wd{i}") for i in range(3)]
        s_xd = [sch.sem(px + "xd0"), sch.sem(px + "xd1")]
        s_gmm = sch.sem(px + "gmm")
        s_umm = sch.sem(px + "umm")
        s_dmm = sch.sem(px + "dmm")
        s_st = [sch.sem(px + "st0"), sch.sem(px + "st1")]
        n = dict(wg=[0, 0], wu=[0, 0], wd=[0, 0, 0], xd=[0, 0], st=[0, 0])

        prev_tok = None
        down_done_tok = None
        wd_mm_tok = [None, None, None]
        silu_tok = [None, None]
        ht_tok = [None, None]
        wg_slot_tok = [None, None]
        wu_slot_tok = [None, None]
        for half in range(2):
            xnT_tok = emit_ln_xnT(sch, nc, L, x1, half * TH, xnT, ident, s_const,
                                  x_rows=TH, pre_tok=prev_tok)
            last_gu = None
            for it in range(NI):
                slot = it % 2
                gbank = psA[:, slot * 512:(slot + 1) * 512]
                ubank = psB[:, slot * 512:(slot + 1) * 512]
                gm = um = None
                for hs in range(2):
                    bslot = (it * 2 + hs) % 2
                    w = [wg_slot_tok[bslot]] if wg_slot_tok[bslot] else []
                    n["wg"][bslot] += 1
                    if gathered:
                        if w_tok and it == 0 and hs == 0 and half == 0:
                            w = list(w) + list(w_tok)
                        iv = (wg_h.rearrange("(r i) p f -> i r p f", r=8)[it, 4 * hs:4 * hs + 4, :, :]
                              .rearrange("r p f -> p r f"))
                        sch.on("sync", lambda E, iv=iv, bslot=bslot: E.dma_start(
                            out=wgbuf[bslot][:, :].rearrange("p (r f) -> p r f", r=4), in_=iv),
                            wait=w, inc=s_wg[bslot], inck=16)
                    else:
                        sch.on("sync", lambda E, it=it, hs=hs, bslot=bslot: E.dma_start(
                            out=wgbuf[bslot][:, :], in_=wg_h[it, :, hs * 2048:(hs + 1) * 2048]),
                            wait=w, inc=s_wg[bslot], inck=16)
                    wgtok = (s_wg[bslot], 16 * n["wg"][bslot])
                    w = [wu_slot_tok[bslot]] if wu_slot_tok[bslot] else []
                    n["wu"][bslot] += 1
                    if gathered:
                        iv = (wu_h.rearrange("(r i) p f -> i r p f", r=8)[it, 4 * hs:4 * hs + 4, :, :]
                              .rearrange("r p f -> p r f"))
                        sch.on("sync", lambda E, iv=iv, bslot=bslot: E.dma_start(
                            out=wubuf[bslot][:, :].rearrange("p (r f) -> p r f", r=4), in_=iv),
                            wait=w, inc=s_wu[bslot], inck=16)
                    else:
                        sch.on("sync", lambda E, it=it, hs=hs, bslot=bslot: E.dma_start(
                            out=wubuf[bslot][:, :], in_=wu_h[it, :, hs * 2048:(hs + 1) * 2048]),
                            wait=w, inc=s_wu[bslot], inck=16)
                    wutok = (s_wu[bslot], 16 * n["wu"][bslot])
                    wq = [wgtok, xnT_tok]
                    if hs == 0 and silu_tok[slot] is not None:
                        wq.append(silu_tok[slot])
                    for jj in range(16):
                        j = hs * 16 + jj
                        gm = sch.on("tensor", lambda E, j=j, jj=jj, bslot=bslot, gbank=gbank: E.matmul(
                            gbank, wgbuf[bslot][:, jj * 128:(jj + 1) * 128], xnT[:, j * TH:(j + 1) * TH],
                            start=(j == 0), stop=(j == ND - 1)),
                            wait=(wq if jj == 0 else []), inc=(s_gmm if jj == 15 else None))
                    wq = [wutok]
                    if hs == 0 and ht_tok[slot] is not None:
                        wq.append(ht_tok[slot])
                    for jj in range(16):
                        j = hs * 16 + jj
                        um = sch.on("tensor", lambda E, j=j, jj=jj, bslot=bslot, ubank=ubank: E.matmul(
                            ubank, wubuf[bslot][:, jj * 128:(jj + 1) * 128], xnT[:, j * TH:(j + 1) * TH],
                            start=(j == 0), stop=(j == ND - 1)),
                            wait=(wq if jj == 0 else []), inc=(s_umm if jj == 15 else None))
                    wg_slot_tok[bslot] = gm
                    wu_slot_tok[bslot] = um
                stok = sch.on("scalar", lambda E, slot=slot, gbank=gbank: E.activation(
                    tmp[slot][:, :], gbank, AF.Silu), wait=[gm])
                silu_tok[slot] = stok
                httok = sch.on("vector", lambda E, it=it, slot=slot, ubank=ubank: E.tensor_tensor(
                    hT[:, it * TH:(it + 1) * TH], tmp[slot][:, :], ubank, OP.mult),
                    wait=[stok, um] + ([down_done_tok] if (it == 0 and down_done_tok) else []))
                ht_tok[slot] = httok
                last_gu = httok
            prev_tok = last_gu
            bank = lambda tcl: (psA if tcl < 2 else psB)[:, (tcl % 2) * 512:(tcl % 2 + 1) * 512]
            ev_tok = None
            dm_last = None
            for dc in range(8):
                xslot = dc % 2
                w = [(s_st[xslot], 16 * n["st"][xslot])] if n["st"][xslot] else []
                n["xd"][xslot] += 1
                sch.on("sync", lambda E, dc=dc, half=half, xslot=xslot: E.dma_start(
                    out=xdb[2 * xslot][:, :].rearrange("p (n c) -> p n c", c=512),
                    in_=x1[half * TH:half * TH + 256, dc * 512:(dc + 1) * 512].rearrange("(n p) c -> p n c", p=128)),
                    wait=w, inc=s_xd[xslot], inck=16)
                sch.on("sync", lambda E, dc=dc, half=half, xslot=xslot: E.dma_start(
                    out=xdb[2 * xslot + 1][:, :].rearrange("p (n c) -> p n c", c=512),
                    in_=x1[half * TH + 256:(half + 1) * TH, dc * 512:(dc + 1) * 512].rearrange("(n p) c -> p n c", p=128)),
                    inc=s_xd[xslot], inck=16)
                n["xd"][xslot] += 1
                xdtok = (s_xd[xslot], 16 * n["xd"][xslot])
                for it in range(NI):
                    slot = it % 3
                    w = [wd_mm_tok[slot]] if wd_mm_tok[slot] else []
                    n["wd"][slot] += 1
                    r0 = ((it // 11) * 1408 + (it % 11) * 128) if gathered else (it * 128)
                    if w_tok and it < 3 and dc == 0 and half == 0:
                        w = list(w) + list(w_tok)
                    if slot % 2 == 0:
                        sch.on("sync", lambda E, r0=r0, dc=dc, slot=slot: E.dma_start(
                            out=wdbuf[slot][:, :], in_=wd[r0:r0 + 128, dc * 512:(dc + 1) * 512]),
                            wait=w, inc=s_wd[slot], inck=16)
                    else:
                        sch.on("gpsimd", lambda E, r0=r0, dc=dc, slot=slot: E.dma_start(
                            out=wdbuf[slot][:, :], in_=wd[r0:r0 + 128, dc * 512:(dc + 1) * 512],
                            max_dma_last_dim=8192),
                            wait=w, inc=s_wd[slot], inck=16, dma=True)
                    wdtok = (s_wd[slot], 16 * n["wd"][slot])
                    wq = [wdtok]
                    if it == 0:
                        wq.append(last_gu)
                        if ev_tok is not None:
                            wq.append(ev_tok)
                    for tcl in range(4):
                        dm_last = sch.on("tensor", lambda E, it=it, tcl=tcl, slot=slot: E.matmul(
                            bank(tcl), hT[:, it * TH + tcl * 128: it * TH + (tcl + 1) * 128],
                            wdbuf[slot][:, :], start=(it == 0), stop=(it == NI - 1)),
                            wait=(wq if tcl == 0 else []), inc=(s_dmm if tcl == 3 else None))
                    wd_mm_tok[slot] = dm_last
                for tcl in range(4):
                    xt = xdb[2 * xslot + tcl // 2]
                    xv = xt[:, (tcl % 2) * 512:(tcl % 2 + 1) * 512]
                    ev_tok = sch.on("vector", lambda E, xv=xv, tcl=tcl: E.tensor_tensor(
                        xv, bank(tcl), xv, OP.add),
                        wait=([dm_last, xdtok] if tcl == 0 else []))
                    n["st"][xslot] += 1
                    sch.on("scalar", lambda E, xv=xv, tcl=tcl, dc=dc, half=half: E.dma_start(
                        out=x2[half * TH + tcl * 128: half * TH + (tcl + 1) * 128, dc * 512:(dc + 1) * 512],
                        in_=xv),
                        wait=[ev_tok], inc=s_st[xslot], inck=16, dma=True)
            down_done_tok = ev_tok
        sch.on("sync", lambda E: E.nop(), wait=[(s_st[0], 16 * n["st"][0]), (s_st[1], 16 * n["st"][1])])


def build_D():
    nc = bass.Bass("TRN2", target_bir_lowering=False, debug=False)
    I = 11008
    NI = I // 128
    TH = 512
    x1 = nc.dram_tensor("x1", [T, D], F32, kind="ExternalInput").ap()
    wg_h = nc.dram_tensor("wg_h", [NI, 128, 4096], BF16, kind="ExternalInput").ap()
    wu_h = nc.dram_tensor("wu_h", [NI, 128, 4096], BF16, kind="ExternalInput").ap()
    wd = nc.dram_tensor("wd", [I, D], BF16, kind="ExternalInput").ap()
    x2 = nc.dram_tensor("x2", [T, D], F32, kind="ExternalOutput").ap()
    sch = Sched()
    with contextlib.ExitStack() as st:
        C = make_consts(sch, nc, st)
        emit_D(sch, nc, st, "d_", C, x1, wg_h, wu_h, wd, x2)
        sch.run(nc)
    return nc



def emit_E(sch, nc, st, px, C, x2, Wq, Wk, Wv, bq, bk, bv, posrow, q1T, k1T, v1, ag_cb=None):
    ident, ones_bf, ones_f32, s_const = C.ident, C.ones_bf, C.ones_f32, C.s_const
    onesrow_f32, ident_f32 = C.onesrow_f32, C.ident_f32
    if True:
        P1 = _enter(st, nc.sbuf_tensor(px + "P1", [128, 128], BF16))
        P2 = _enter(st, nc.sbuf_tensor(px + "P2", [128, 128], BF16))
        Rm = _enter(st, nc.sbuf_tensor(px + "Rm", [128, 128], BF16))
        sch.on("gpsimd", lambda E: E.affine_select(P1[:, :], ones_bf[:, :], pattern=[[-1, 128]],
                                                   compare_op=OP.is_equal, fill=0.0, base=64, channel_multiplier=1))
        t_p2 = sch.on("gpsimd", lambda E: E.affine_select(P2[:, :], ones_bf[:, :], pattern=[[-1, 128]],
                                                          compare_op=OP.is_equal, fill=0.0, base=-64, channel_multiplier=1))
        t_R = sch.on("vector", lambda E: E.tensor_tensor(Rm[:, :], P1[:, :], P2[:, :], OP.subtract), wait=[t_p2])

        ps_tp = [_enter(st, nc.psum_tensor(px + f"pstp{i}", [128, 1024], BF16)) for i in range(2)]
        psA = _enter(st, nc.psum_tensor(px + "psA", [128, 1024], F32))
        psB = _enter(st, nc.psum_tensor(px + "psB", [128, 1024], F32))
        psR = _enter(st, nc.psum_tensor(px + "psR", [128, 1024], F32))
        xnT = _enter(st, nc.sbuf_tensor(px + "xnT", [128, ND * T], BF16))
        L = LNSet(sch, nc, st, px + "ln", ps_tp)
        xnT_tok = emit_ln_xnT(sch, nc, L, x2, 0, xnT, ident, s_const)

        pos_t = _enter(st, nc.sbuf_tensor(px + "pos_t", [128, T], F32))
        thv = _enter(st, nc.sbuf_tensor(px + "thv", [128, 1], F32))
        ang = _enter(st, nc.sbuf_tensor(px + "ang", [128, T], F32))
        angc = _enter(st, nc.sbuf_tensor(px + "angc", [128, T], F32))
        scr_i = _enter(st, nc.sbuf_tensor(px + "scr_i", [128, T], I32))
        scr_f = _enter(st, nc.sbuf_tensor(px + "scr_f", [128, T], F32))
        cmpb = _enter(st, nc.sbuf_tensor(px + "cmpb", [128, T], F32))
        sin2 = _enter(st, nc.sbuf_tensor(px + "sin2", [128, T], F32))
        cos2 = _enter(st, nc.sbuf_tensor(px + "cos2", [128, T], F32))
        s_pos = sch.sem(px + "pos")
        sch.on("sync", lambda E: E.dma_start(out=pos_t[0:1, :], in_=posrow[:, :]), inc=s_pos, inck=16)
        w = [(s_pos, 16), (s_const, s_const.n)]
        bc = None
        for half in range(2):
            bc = sch.on("tensor", lambda E, half=half: E.matmul(
                psR[:, half * 512:(half + 1) * 512], onesrow_f32[:, :], pos_t[0:1, half * 512:(half + 1) * 512],
                start=True, stop=True), wait=(w if half == 0 else []), inc=(sch.sem(px + "posbc") if half else None))
        pt_tok = sch.on("scalar", lambda E: E.copy(pos_t[:, :], psR[:, :]), wait=[bc])
        it_tok = sch.on("gpsimd", lambda E: E.iota(thv[:, :], pattern=[[0, 1]], base=0, channel_multiplier=1,
                                                   allow_small_or_imprecise_dtypes=True))
        c_tok = sch.on("vector", lambda E: E.tensor_scalar(cmpb[:, 0:1], thv[:, :], 64.0, None, OP.is_ge), wait=[it_tok])
        sch.on("vector", lambda E: E.scalar_tensor_tensor(thv[:, :], cmpb[:, 0:1], -64.0, thv[:, :], OP.mult, OP.add))
        sch.on("vector", lambda E: E.tensor_scalar(thv[:, :], thv[:, :], -0.14391156463214672, None, OP.mult))
        th_tok = sch.on("scalar", lambda E: E.activation(thv[:, :], thv[:, :], AF.Exp), wait=[sch.last["vector"]])
        ang_tok = sch.on("vector", lambda E: E.tensor_scalar(ang[:, :], pos_t[:, :], thv[:, :], None, OP.mult),
                         wait=[th_tok, pt_tok])

        TWO_PI = 6.283185307179586
        def range_reduce_sin(src_t, dst_t, phase, war=None):
            t0 = sch.on("vector", lambda E: E.tensor_scalar(angc[:, :], src_t[:, :], phase, None, OP.add)) if phase else None
            s = angc if phase else src_t
            sch.on("vector", lambda E, s=s: E.tensor_scalar(scr_f[:, :], s[:, :], 1.0 / TWO_PI, None, OP.mult),
                   wait=([war] if war else []))
            sch.on("vector", lambda E: E.tensor_copy(scr_i[:, :], scr_f[:, :]))
            sch.on("vector", lambda E: E.tensor_copy(scr_f[:, :], scr_i[:, :]))
            sch.on("vector", lambda E, s=s: E.scalar_tensor_tensor(scr_f[:, :], scr_f[:, :], -TWO_PI, s[:, :], OP.mult, OP.add))
            sch.on("vector", lambda E: E.tensor_scalar(cmpb[:, :], scr_f[:, :], 3.14159265358979, None, OP.is_gt))
            sch.on("vector", lambda E: E.scalar_tensor_tensor(scr_f[:, :], cmpb[:, :], -TWO_PI, scr_f[:, :], OP.mult, OP.add))
            sch.on("vector", lambda E: E.tensor_scalar(cmpb[:, :], scr_f[:, :], -3.14159265358979, None, OP.is_lt))
            v = sch.on("vector", lambda E: E.scalar_tensor_tensor(scr_f[:, :], cmpb[:, :], TWO_PI, scr_f[:, :], OP.mult, OP.add))
            return sch.on("scalar", lambda E, dst_t=dst_t: E.activation(dst_t[:, :], scr_f[:, :], AF.Sin), wait=[v])

        sin_tok = range_reduce_sin(ang, sin2, 0.0)
        cos_tok = range_reduce_sin(ang, cos2, 1.5707963267948966, war=sin_tok)

        s_bias = sch.sem(px + "bias")
        bq_sb = dma_bias_cols(sch, nc, st, bq, 4096, px + "bq_sb", s_bias)
        bk_sb = dma_bias_cols(sch, nc, st, bk, 1024, px + "bk_sb", s_bias)
        bias_total = s_bias.n
        bv_row = _enter(st, nc.sbuf_tensor(px + "bvrow", [1, 1024], BF16))
        s_bvr = sch.sem(px + "bvr")
        sch.on("gpsimd", lambda E: E.dma_start(out=bv_row[:, :], in_=bv.rearrange("(o n) -> o n", o=1),
                                               max_dma_last_dim=8192), inc=s_bvr, inck=16, dma=True)

        wbuf = [_enter(st, nc.sbuf_tensor(px + f"wb{i}", [128, 8192], BF16)) for i in range(2)]
        qraw = [_enter(st, nc.sbuf_tensor(px + f"qraw{i}", [128, T], BF16)) for i in range(2)]
        ta = _enter(st, nc.sbuf_tensor(px + "ta", [128, T], F32))
        outs = _enter(st, nc.sbuf_tensor(px + "outs", [128, 2 * T], BF16))
        s_w = [sch.sem(px + "w0"), sch.sem(px + "w1")]
        s_mmw = sch.sem(px + "mmw")
        s_ev = sch.sem(px + "evp")
        s_rot = sch.sem(px + "rot")
        s_st = [sch.sem(px + "stq0"), sch.sem(px + "stq1")]
        n_w = [0, 0]
        n_st = [0, 0]
        wslot_tok = [None, None]
        bank_tok = [None, None]
        rot_tok = None
        outs_tok = [None, None]
        qraw_tok = [None, None]

        def proj_wave(W_h, ncols_all, tiles, kind, bias_sb=None, et0_bias=0, dst_hbm=None):
            nonlocal rot_tok
            pair = [psA, psB]
            for wi, et in enumerate(tiles):
                pass
            b = (n_w[0] + n_w[1]) % 2
            w = [wslot_tok[b]] if wslot_tok[b] else []
            n_w[b] += 1
            cw = 128 * len(tiles)
            sch.on("gpsimd", lambda E, b=b, W_h=W_h, tiles=tiles, cw=cw: E.dma_start(
                out=wbuf[b][:, 0:ND * cw].rearrange("p (j c) -> p j c", c=cw),
                in_=W_h[:, tiles[0] * 128:tiles[0] * 128 + cw].rearrange("(j p) c -> p j c", p=128),
                max_dma_last_dim=8192),
                wait=w, inc=s_w[b], inck=16, dma=True)
            wtok = (s_w[b], 16 * n_w[b])
            mmtok = None
            for j in range(ND):
                wq0 = [wtok, xnT_tok]
                if j == 0:
                    for t_ in bank_tok:
                        if t_:
                            wq0.append(t_)
                for wi in range(len(tiles)):
                    for half in range(2):
                        mmtok = sch.on("tensor", lambda E, j=j, wi=wi, half=half, b=b, cw=cw: E.matmul(
                            pair[wi][:, half * 512:(half + 1) * 512],
                            wbuf[b][:, j * cw + wi * 128: j * cw + (wi + 1) * 128],
                            xnT[:, j * T + half * 512: j * T + (half + 1) * 512],
                            start=(j == 0), stop=(j == ND - 1)),
                            wait=(wq0 if (wi == 0 and half == 0) else []),
                            inc=(s_mmw if (j == ND - 1 and wi == len(tiles) - 1 and half == 1) else None))
            wslot_tok[b] = mmtok
            mmdone = (s_mmw, s_mmw.n)
            for wi, et in enumerate(tiles):
                qb = qraw[wi % 2]
                w = [mmdone, (s_bias, bias_total)]
                if qraw_tok[wi % 2]:
                    w.append(qraw_tok[wi % 2])
                ev = sch.on("scalar", lambda E, wi=wi, et=et, qb=qb: E.activation(
                    qb[:, :], pair[wi][:, :], AF.Identity, bias=bias_sb[:, et0_bias + et:et0_bias + et + 1]),
                    wait=w, inc=s_ev)
                w = [ev, t_R]
                if rot_tok:
                    w.append(rot_tok)
                rtok = None
                for half in range(2):
                    rtok = sch.on("tensor", lambda E, half=half, qb=qb: E.matmul(
                        psR[:, half * 512:(half + 1) * 512], Rm[:, :], qb[:, half * 512:(half + 1) * 512],
                        start=True, stop=True), wait=(w if half == 0 else []),
                        inc=(s_rot if half == 1 else None))
                ov = outs[:, (wi % 2) * T:(wi % 2 + 1) * T]
                w2 = [cos_tok, ev] + ([outs_tok[wi % 2]] if outs_tok[wi % 2] else [])
                v1_ = sch.on("vector", lambda E, qb=qb: E.tensor_tensor(ta[:, :], qb[:, :], cos2[:, :], OP.mult),
                             wait=w2)
                v2_ = sch.on("vector", lambda E, ov=ov: E.tensor_tensor(ov, psR[:, :], sin2[:, :], OP.mult),
                             wait=[rtok, sin_tok])
                rot_tok = v2_
                v3_ = sch.on("vector", lambda E, ov=ov: E.tensor_tensor(ov, ov, ta[:, :], OP.add))
                qraw_tok[wi % 2] = v3_
                bank_tok[wi] = v3_
                sl = wi % 2
                n_st[sl] += 1
                sch.on("sync", lambda E, et=et, ov=ov, dst=dst_hbm: E.dma_start(out=dst[et * 128:(et + 1) * 128, :], in_=ov),
                       wait=[v3_], inc=s_st[sl], inck=16)
                outs_tok[sl] = (s_st[sl], 16 * n_st[sl])

        for w0 in range(0, 8, 2):
            proj_wave(Wk, 1024, [w0, w0 + 1], "rope", bk_sb, 0, dst_hbm=k1T)

        s_wv = [sch.sem(px + "wv0"), sch.sem(px + "wv1")]
        n_wv = [0, 0]
        wv_slot_tok = [wslot_tok[0], wslot_tok[1]]
        pair = [psA, psB]
        for w0 in range(0, 8, 2):
            mmtok = None
            for jc in range(4):
                b = jc % 2
                w = [wv_slot_tok[b]] if wv_slot_tok[b] else []
                n_wv[b] += 1
                sch.on("gpsimd", lambda E, jc=jc, b=b: E.dma_start(
                    out=wbuf[b][:, :].rearrange("p (j c) -> p j c", c=1024),
                    in_=Wv[jc * 1024:(jc + 1) * 1024, :].rearrange("(j p) c -> p j c", p=128),
                    max_dma_last_dim=8192),
                    wait=w, inc=s_wv[b], inck=16, dma=True)
                wtok = (s_wv[b], 16 * n_wv[b])
                for jj in range(8):
                    j = jc * 8 + jj
                    wq0 = [wtok, xnT_tok]
                    if j == 0:
                        for t_ in bank_tok:
                            if t_:
                                wq0.append(t_)
                    for wi in range(2):
                        tc = w0 + wi
                        for half in range(2):
                            mmtok = sch.on("tensor", lambda E, j=j, jj=jj, wi=wi, tc=tc, half=half, b=b: E.matmul(
                                pair[wi][:, half * 512:(half + 1) * 512],
                                xnT[:, j * T + tc * 128: j * T + (tc + 1) * 128],
                                wbuf[b][:, jj * 1024 + half * 512: jj * 1024 + (half + 1) * 512],
                                start=(j == 0), stop=False),
                                wait=(wq0 if (wi == 0 and half == 0) else []),
                                inc=(s_mmw if (jj == 7 and wi == 1 and half == 1) else None))
                wv_slot_tok[b] = mmtok
            for wi in range(2):
                for half in range(2):
                    mmtok = sch.on("tensor", lambda E, wi=wi, half=half: E.matmul(
                        pair[wi][:, half * 512:(half + 1) * 512], ones_bf[0:1, :],
                        bv_row[:, half * 512:(half + 1) * 512], start=False, stop=True),
                        wait=([(s_bvr, 16)] if (wi == 0 and half == 0) else []),
                        inc=(s_mmw if (wi == 1 and half == 1) else None))
            mmdone = (s_mmw, s_mmw.n)
            for wi in range(2):
                tc = w0 + wi
                ov = outs[:, (wi % 2) * T:(wi % 2 + 1) * T]
                w = [mmdone] + ([outs_tok[wi % 2]] if outs_tok[wi % 2] else [])
                ev = sch.on("scalar", lambda E, wi=wi, ov=ov: E.copy(ov, pair[wi][:, :]), wait=w)
                bank_tok[wi] = ev
                sl = wi % 2
                n_st[sl] += 1
                sch.on("sync", lambda E, tc=tc, ov=ov: E.dma_start(out=v1[tc * 128:(tc + 1) * 128, :], in_=ov),
                       wait=[ev], inc=s_st[sl], inck=16)
                outs_tok[sl] = (s_st[sl], 16 * n_st[sl])
        kv_toks = [(s_st[0], 16 * n_st[0]), (s_st[1], 16 * n_st[1])]
        wslot_tok[0] = wv_slot_tok[0]
        wslot_tok[1] = wv_slot_tok[1]
        for w0 in range(0, 32, 2):
            proj_wave(Wq, 4096, [w0, w0 + 1], "rope", bq_sb, 0, dst_hbm=q1T)
        if ag_cb is not None:
            ag_cb(kv_toks)
        sch.on("sync", lambda E: E.nop(), wait=[(s_st[0], 16 * n_st[0]), (s_st[1], 16 * n_st[1])])
    return [(s_st[0], 16 * n_st[0]), (s_st[1], 16 * n_st[1])]


def build_E():
    nc = bass.Bass("TRN2", target_bir_lowering=False, debug=False)
    x2 = nc.dram_tensor("x2", [T, D], F32, kind="ExternalInput").ap()
    Wq = nc.dram_tensor("Wq", [D, 4096], F32, kind="ExternalInput").ap()
    Wk = nc.dram_tensor("Wk", [D, 1024], F32, kind="ExternalInput").ap()
    Wv = nc.dram_tensor("Wv", [D, 1024], F32, kind="ExternalInput").ap()
    bq = nc.dram_tensor("bq", [4096], F32, kind="ExternalInput").ap()
    bk = nc.dram_tensor("bk", [1024], F32, kind="ExternalInput").ap()
    bv = nc.dram_tensor("bv", [1024], F32, kind="ExternalInput").ap()
    posrow = nc.dram_tensor("posrow", [1, T], F32, kind="ExternalInput").ap()
    q1T = nc.dram_tensor("q1T", [4096, T], BF16, kind="ExternalOutput").ap()
    k1T = nc.dram_tensor("k1T", [1024, T], BF16, kind="ExternalOutput").ap()
    v1 = nc.dram_tensor("v1", [T, 1024], BF16, kind="ExternalOutput").ap()
    sch = Sched()
    with contextlib.ExitStack() as st:
        C = make_consts(sch, nc, st)
        emit_E(sch, nc, st, "e_", C, x2, Wq, Wk, Wv, bq, bk, bv, posrow, q1T, k1T, v1)
        sch.run(nc)
    return nc



def emit_F(sch, nc, st, px, C, q1T, kTr, v1g, x2, Wo, bo, visrow, x3, gathered=False, kv_tok=None):
    TK = 4096
    NKT = 32
    SCALE = 0.08838834764831845
    ident, ones_bf, ones_f32, s_const = C.ident, C.ones_bf, C.ones_f32, C.s_const
    onesrow_f32, ident_f32 = C.onesrow_f32, C.ident_f32
    if True:
        qT_sb = _enter(st, nc.sbuf_tensor(px + "qTsb", [128, 32 * T], BF16))
        ctxT_sb = _enter(st, nc.sbuf_tensor(px + "ctxTsb", [128, 32 * T], BF16))
        kgbuf = [_enter(st, nc.sbuf_tensor(px + "kgb0", [128, TK], BF16))] * 2
        vgbuf = [_enter(st, nc.sbuf_tensor(px + "vgb0", [128, TK], BF16))] * 2
        den_acc = _enter(st, nc.sbuf_tensor(px + "den", [128, T], F32))
        den_acc2 = _enter(st, nc.sbuf_tensor(px + "den2", [128, T], F32))
        expT = [_enter(st, nc.sbuf_tensor(px + f"expT{i}", [128, T], BF16)) for i in range(2)]
        expm = [_enter(st, nc.sbuf_tensor(px + f"expm{i}", [128, T], BF16)) for i in range(2)]
        fmx = _enter(st, nc.sbuf_tensor(px + "fmx", [128, 1], F32))
        negm = _enter(st, nc.sbuf_tensor(px + "negm", [1, 1], F32))
        mbias = _enter(st, nc.sbuf_tensor(px + "mbias", [128, 1], F32))
        vis_rep = _enter(st, nc.sbuf_tensor(px + "vis_rep", [128, NKT], F32))
        biastab = _enter(st, nc.sbuf_tensor(px + "biastab", [128, NKT], F32))
        dbc_sb = _enter(st, nc.sbuf_tensor(px + "dbc", [128, T], F32))
        bo_row = _enter(st, nc.sbuf_tensor(px + "borow", [1, D], BF16))

        ps_s = _enter(st, nc.psum_tensor(px + "ps_s", [128, 1024], F32))
        ps_c = _enter(st, nc.psum_tensor(px + "ps_c", [128, 1024], F32))
        ps_m = _enter(st, nc.psum_tensor(px + "ps_m", [128, 1024], F32))
        ps_o = _enter(st, nc.psum_tensor(px + "ps_o", [128, 1024], F32))
        pbanks = [ps_s, ps_c, ps_m, ps_o]

        s_ld = sch.sem(px + "ld")
        nld = 0
        for e in range(32):
            sch.on("sync", lambda E, e=e: E.dma_start(out=qT_sb[:, e * T:(e + 1) * T], in_=q1T[e * 128:(e + 1) * 128, :]),
                   inc=s_ld, inck=16); nld += 1
        sch.on("sync", lambda E: E.dma_start(out=vis_rep[0:1, :], in_=visrow[:, :]), inc=s_ld, inck=16); nld += 1
        ld_tot = 16 * nld
        vtok = sch.on("tensor", lambda E: E.matmul(ps_m[:, 0:NKT], onesrow_f32[:, :], vis_rep[0:1, :],
                                                   start=True, stop=True),
                      wait=[(s_ld, ld_tot), (s_const, s_const.n)], inc=sch.sem(px + "visbc"))
        vis_tok = sch.on("scalar", lambda E: E.copy(vis_rep[:, :], ps_m[:, 0:NKT]), wait=[vtok])

        s_kg = [sch.sem(px + "kg0"), sch.sem(px + "kg1")]
        s_vg = [sch.sem(px + "vg0"), sch.sem(px + "vg1")]
        n_kg = [0, 0]
        n_vg = [0, 0]
        kg_tok = [None, None]
        vg_tok = [None, None]
        s_sc = sch.sem(px + "sc"); s_pv = sch.sem(px + "pv"); s_mb = sch.sem(px + "mbm")
        s_dnm = sch.sem(px + "dnm"); s_bcm = sch.sem(px + "bcm"); s_ev = sch.sem(px + "ev")
        exp_tok = [None, None]
        denpv_tok = [None, None]
        den_tok = [None, None]
        dnf_tok = [None]
        evac_tok = None
        mcopy_tok = None
        dbc_tok = None
        bt_tok = None
        aff_tok = None
        for g in range(8):
            gslot = 0
            if not gathered:
                w = [kg_tok[gslot]] if kg_tok[gslot] else []
                n_kg[gslot] += 1
                sch.on("sync", lambda E, g=g, gslot=gslot: E.dma_start(out=kgbuf[gslot][:, :], in_=kTr[g * 128:(g + 1) * 128, :]),
                       wait=w, inc=s_kg[gslot], inck=16)
                kgtok = (s_kg[gslot], 16 * n_kg[gslot])
                w = [vg_tok[gslot]] if vg_tok[gslot] else []
                n_vg[gslot] += 1
                sch.on("sync", lambda E, g=g, gslot=gslot: E.dma_start(
                    out=vgbuf[gslot][:, :].rearrange("p (s e) -> p s e", e=128),
                    in_=v1g[g, :, :].rearrange("(s p) e -> p s e", p=128)),
                    wait=w, inc=s_vg[gslot], inck=16)
                vgtok = (s_vg[gslot], 16 * n_vg[gslot])
            else:
                w = ([kg_tok[gslot]] if kg_tok[gslot] else []) + ([vg_tok[gslot]] if vg_tok[gslot] else [])
                if kv_tok and g == 0:
                    w = w + list(kv_tok)

                def kvload(E, g=g, gslot=gslot):
                    for arm in E.switch_core_id(n=128):
                        r = arm.logical % 4
                        for i in range(4):
                            sr = (r + i) % 4
                            E.dma_start(out=kgbuf[gslot][:, i * 1024:(i + 1) * 1024],
                                        in_=kTr[sr * 1024 + g * 128: sr * 1024 + (g + 1) * 128, :]
                                        ).then_inc(s_kg[gslot].h, 16)
                            E.dma_start(out=vgbuf[gslot][:, i * 1024:(i + 1) * 1024].rearrange("p (s e) -> p s e", e=128),
                                        in_=v1g[sr * 1024:(sr + 1) * 1024, g * 128:(g + 1) * 128]
                                        .rearrange("(s p) e -> p s e", p=128)
                                        ).then_inc(s_vg[gslot].h, 16)
                    return None
                sch.on("sync", kvload, wait=w)
                n_kg[gslot] += 4
                n_vg[gslot] += 4
                s_kg[gslot].n += 64
                s_vg[gslot].n += 64
                kgtok = (s_kg[gslot], 16 * n_kg[gslot])
                vgtok = (s_vg[gslot], 16 * n_vg[gslot])
            sc_toks = {}

            def f_scores(s_, kgtok, h, gslot):
                b = s_ % 2
                spair = ps_s if b == 0 else ps_o
                w = [kgtok]
                if s_ == 0:
                    w.append((s_ld, ld_tot))
                if exp_tok[b] is not None:
                    w.append(exp_tok[b])
                tok = None
                for half in range(2):
                    tok = sch.on("tensor", lambda E, h=h, s_=s_, half=half, gslot=gslot, spair=spair: E.matmul(
                        spair[:, half * 512:(half + 1) * 512],
                        kgbuf[gslot][:, s_ * 128:(s_ + 1) * 128],
                        qT_sb[:, h * T + half * 512: h * T + (half + 1) * 512],
                        start=True, stop=True),
                        wait=(w if half == 0 else []), inc=(s_sc if half == 1 else None))
                sc_toks[s_] = tok

            def f_tail(s_, vgtok, gslot):
                nonlocal aff_tok
                b = s_ % 2
                spair = ps_s if b == 0 else ps_o
                w = [sc_toks[s_], bt_tok]
                if denpv_tok[b] is not None:
                    w.append(denpv_tok[b])
                if den_tok[b] is not None:
                    w.append(den_tok[b])
                if s_ < 8:
                    etok = sch.on("scalar", lambda E, b=b, s_=s_, spair=spair: E.activation(
                        expm[b][:, :], spair[:, :], AF.Exp, bias=biastab[:, s_:s_ + 1], scale=SCALE), wait=w)
                    aff_tok = exp_tok[b] = sch.on("gpsimd", lambda E, b=b, s_=s_: E.affine_select(
                        expT[b][:, :], expm[b][:, :], pattern=[[1, T]], compare_op=OP.is_ge,
                        fill=0.0, base=-128 * s_, channel_multiplier=-1), wait=[etok])
                else:
                    exp_tok[b] = sch.on("scalar", lambda E, b=b, s_=s_, spair=spair: E.activation(
                        expT[b][:, :], spair[:, :], AF.Exp, bias=biastab[:, s_:s_ + 1], scale=SCALE), wait=w)
                deng = "vector" if b == 0 else "gpsimd"
                dacc = den_acc if b == 0 else den_acc2
                if s_ <= 1:
                    wd_ = [exp_tok[b]] + ([dnf_tok[0]] if (b == 1 and dnf_tok[0]) else [])
                    den_tok[b] = sch.on(deng, lambda E, b=b, dacc=dacc: E.tensor_copy(dacc[:, :], expT[b][:, :]),
                                        wait=wd_)
                else:
                    den_tok[b] = sch.on(deng, lambda E, b=b, dacc=dacc: E.tensor_tensor(
                        dacc[:, :], dacc[:, :], expT[b][:, :], OP.add), wait=[exp_tok[b]])
                w = [exp_tok[b]]
                if s_ == 0:
                    w.append(vgtok)
                    if evac_tok is not None:
                        w.append(evac_tok)
                for half in range(2):
                    denpv_tok[b] = sch.on("tensor", lambda E, s_=s_, half=half, gslot=gslot, b=b: E.matmul(
                        ps_c[:, half * 512:(half + 1) * 512],
                        vgbuf[gslot][:, s_ * 128:(s_ + 1) * 128],
                        expT[b][:, half * 512:(half + 1) * 512],
                        start=(s_ == 0), stop=(s_ == NKT - 1)),
                        wait=(w if half == 0 else []), inc=(s_pv if half == 1 else None))

            for qh in range(4):
                h = g * 4 + qh
                for s_ in range(NKT):
                    f_scores(s_, kgtok, h, gslot)
                    if s_ == 0:
                        t0 = sch.on("vector", lambda E: E.tensor_reduce(fmx[:, :], ps_s[:, :], AX.X, OP.max),
                                    wait=[sc_toks[0]])
                        w2 = [t0, (s_const, s_const.n), vis_tok]
                        if mcopy_tok is not None:
                            w2.append(mcopy_tok)
                        if dbc_tok is not None:
                            w2.append(dbc_tok)
                        tpm_tok = sch.on("tensor", lambda E: E.transpose(
                            ps_m[0:1, 0:128], fmx[:, 0:1], ident_f32[:, :]), wait=w2, inc=s_mb)
                        vr_tok = sch.on("vector", lambda E: E.tensor_reduce(negm[:, :], ps_m[0:1, 0:128], AX.X, OP.max),
                                        wait=[tpm_tok])
                        vm_tok = sch.on("vector", lambda E: E.tensor_scalar(negm[:, :], negm[:, :], -SCALE, -10.0, OP.mult, OP.add))
                        mb_tok = sch.on("tensor", lambda E: E.matmul(
                            ps_m[:, 0:1], onesrow_f32[:, :], negm[:, :], start=True, stop=True),
                            wait=[vm_tok], inc=s_mb)
                        mcopy_tok = sch.on("scalar", lambda E: E.copy(mbias[:, :], ps_m[:, 0:1]), wait=[mb_tok])
                        bt_tok = sch.on("vector", lambda E: E.tensor_scalar(
                            biastab[:, :], vis_rep[:, :], mbias[:, 0:1], None, OP.add),
                            wait=[mcopy_tok, vis_tok] + ([aff_tok] if aff_tok else []))
                    if s_ >= 1:
                        f_tail(s_ - 1, vgtok, gslot)
                f_tail(NKT - 1, vgtok, gslot)
                w = [den_tok[0], den_tok[1], mcopy_tok]
                dn_tok = None
                for half in range(2):
                    for di, dacc in enumerate((den_acc, den_acc2)):
                        dn_tok = sch.on("tensor", lambda E, half=half, dacc=dacc, di=di: E.matmul(
                            ps_m[0:1, half * 512:(half + 1) * 512], ones_f32[:, :],
                            dacc[:, half * 512:(half + 1) * 512], start=(di == 0), stop=(di == 1)),
                            wait=(w if (half == 0 and di == 0) else []),
                            inc=(s_dnm if (half == 1 and di == 1) else None))
                dnf_tok[0] = dn_tok
                r1 = sch.on("vector", lambda E: E.reciprocal(den_acc[0:1, :], ps_m[0:1, :]), wait=[dn_tok])
                w = [r1, (s_const, s_const.n)]
                bc_tok = None
                for half in range(2):
                    bc_tok = sch.on("tensor", lambda E, half=half: E.matmul(
                        ps_m[:, half * 512:(half + 1) * 512], onesrow_f32[:, :],
                        den_acc[0:1, half * 512:(half + 1) * 512], start=True, stop=True),
                        wait=(w if half == 0 else []), inc=(s_bcm if half == 1 else None))
                dbc_tok = sch.on("scalar", lambda E: E.copy(dbc_sb[:, :], ps_m[:, :]), wait=[bc_tok])
                evac_tok = sch.on("vector", lambda E, h=h: E.tensor_tensor(
                    ctxT_sb[:, h * T:(h + 1) * T], ps_c[:, :], dbc_sb[:, :], OP.mult),
                    wait=[denpv_tok[(NKT - 1) % 2], dbc_tok], inc=s_ev)
            kg_tok[gslot] = exp_tok[(NKT - 1) % 2]
            vg_tok[gslot] = denpv_tok[(NKT - 1) % 2]

        s_bor = sch.sem(px + "bor")
        sch.on("gpsimd", lambda E: E.dma_start(out=bo_row[:, :], in_=bo.rearrange("(o n) -> o n", o=1),
                                               max_dma_last_dim=8192), inc=s_bor, inck=16, dma=True)
        wo_buf = [_enter(st, nc.sbuf_tensor(px + f"wob{i}", [128, 4096], BF16)) for i in range(2)]
        x_buf = [_enter(st, nc.sbuf_tensor(px + "xb0", [128, 8 * 512], F32))] * 2
        s_wo = [sch.sem(px + "wo0"), sch.sem(px + "wo1")]
        s_xi = [sch.sem(px + "xi0"), sch.sem(px + "xi1")]
        s_st = [sch.sem(px + "st0"), sch.sem(px + "st1")]
        nst_slot = [0, 0]
        s_wom = sch.sem(px + "wom"); s_mme = sch.sem(px + "mme")
        wo_slot_tok = [None, None]
        x_slot_tok = [None, None]
        allfree = [evac_tok, dbc_tok, mcopy_tok, exp_tok[0], exp_tok[1]]
        ndma_wo = 0
        bank = lambda tc: pbanks[tc // 2][:, (tc % 2) * 512:(tc % 2 + 1) * 512]
        evl_tok = {}
        for dc in range(8):
            xb = 0
            w = [x_slot_tok[xb]] if x_slot_tok[xb] else []
            sch.on("sync", lambda E, dc=dc, xb=xb: E.dma_start(
                out=x_buf[xb][:, :].rearrange("p (n c) -> p n c", c=512),
                in_=x2[:, dc * 512:(dc + 1) * 512].rearrange("(n p) c -> p n c", p=128)),
                wait=w, inc=s_xi[xb], inck=16)
            xin_tok = (s_xi[xb], 16 * (dc + 1))
            for sc_ in range(4):
                wb = ndma_wo % 2
                w = [wo_slot_tok[wb]] if wo_slot_tok[wb] else []
                sch.on("gpsimd", lambda E, sc_=sc_, dc=dc, wb=wb: E.dma_start(
                    out=wo_buf[wb][:, :].rearrange("p (s c) -> p s c", c=512),
                    in_=Wo[sc_ * 1024:(sc_ + 1) * 1024, dc * 512:(dc + 1) * 512].rearrange("(s p) c -> p s c", p=128),
                    max_dma_last_dim=8192), wait=w, inc=s_wo[wb], inck=16, dma=True)
                wtok = (s_wo[wb], 16 * (ndma_wo // 2 + 1))
                for ss in range(8):
                    s_ = sc_ * 8 + ss
                    for tc in range(8):
                        w2 = []
                        if tc == 0:
                            w2 = [wtok] + ([t for t in allfree if t] if (dc == 0 and s_ == 0) else [])
                            if s_ == 0 and dc > 0:
                                w2 += [evl_tok[dc - 1]]
                        last = sch.on("tensor", lambda E, s_=s_, ss=ss, tc=tc, wb=wb: E.matmul(
                            bank(tc), ctxT_sb[:, s_ * T + tc * 128: s_ * T + (tc + 1) * 128],
                            wo_buf[wb][:, ss * 512:(ss + 1) * 512], start=(s_ == 0), stop=False),
                            wait=w2, inc=(s_wom if tc == 7 else None))
                        if tc == 7:
                            wo_slot_tok[wb] = last
                ndma_wo += 1
            mmend_tok = None
            for tc in range(8):
                tk_ = sch.on("tensor", lambda E, tc=tc, dc=dc: E.matmul(
                    bank(tc), ones_bf[0:1, :], bo_row[:, dc * 512:(dc + 1) * 512],
                    start=False, stop=True),
                    wait=([(s_bor, 16)] if tc == 0 else []), inc=(s_mme if tc == 7 else None))
                if tc == 7:
                    mmend_tok = tk_
            ev_last = None
            for tc in range(8):
                ev_last = sch.on("vector", lambda E, tc=tc, xb=xb: E.tensor_tensor(
                    x_buf[xb][:, tc * 512:(tc + 1) * 512], bank(tc),
                    x_buf[xb][:, tc * 512:(tc + 1) * 512], OP.add),
                    wait=([mmend_tok, xin_tok] if tc == 0 else []))
                sch.on("sync", lambda E, tc=tc, dc=dc, xb=xb: E.dma_start(
                    out=x3[tc * 128:(tc + 1) * 128, dc * 512:(dc + 1) * 512],
                    in_=x_buf[xb][:, tc * 512:(tc + 1) * 512]),
                    wait=[ev_last], inc=s_st[xb], inck=16)
                nst_slot[xb] += 1
            evl_tok[dc] = ev_last
            x_slot_tok[xb] = (s_st[xb], 16 * nst_slot[xb])
        sch.on("sync", lambda E: E.nop(), wait=[(s_st[0], 16 * nst_slot[0]), (s_st[1], 16 * nst_slot[1])])


def build_F():
    nc = bass.Bass("TRN2", target_bir_lowering=False, debug=False)
    TK = 4096
    NKT = 32
    q1T = nc.dram_tensor("q1T", [4096, T], BF16, kind="ExternalInput").ap()
    kTr = nc.dram_tensor("kTr", [1024, TK], BF16, kind="ExternalInput").ap()
    v1g = nc.dram_tensor("v1g", [8, TK, 128], BF16, kind="ExternalInput").ap()
    x2 = nc.dram_tensor("x2", [T, D], F32, kind="ExternalInput").ap()
    Wo = nc.dram_tensor("Wo", [4096, D], F32, kind="ExternalInput").ap()
    bo = nc.dram_tensor("bo", [D], F32, kind="ExternalInput").ap()
    visrow = nc.dram_tensor("visrow", [1, NKT], F32, kind="ExternalInput").ap()
    x3 = nc.dram_tensor("x3", [T, D], F32, kind="ExternalOutput").ap()
    SCALE = 0.08838834764831845
    sch = Sched()
    with contextlib.ExitStack() as st:
        C = make_consts(sch, nc, st)
        emit_F(sch, nc, st, "f_", C, q1T, kTr, v1g, x2, Wo, bo, visrow, x3)
        sch.run(nc)
    return nc



def emit_G(sch, nc, st, px, C, x3, WbT, bb, wr_t, AallT, BallT, x4):
    ident, ones_bf, ones_f32, s_const = C.ident, C.ones_bf, C.ones_f32, C.s_const
    onesrow_f32, ident_f32 = C.onesrow_f32, C.ident_f32
    if True:
        ps_tp = [_enter(st, nc.psum_tensor(px + f"pstp{i}", [128, 1024], BF16)) for i in range(2)]
        psA = _enter(st, nc.psum_tensor(px + "psA", [128, 1024], F32))
        psB = _enter(st, nc.psum_tensor(px + "psB", [128, 1024], F32))
        psC = _enter(st, nc.psum_tensor(px + "psC", [128, 1024], F32))
        xnT = _enter(st, nc.sbuf_tensor(px + "xnT", [128, ND * T], BF16))
        L = LNSet(sch, nc, st, px + "ln", ps_tp)
        xnT_tok = emit_ln_xnT(sch, nc, L, x3, 0, xnT, ident, s_const)

        wr_sb = _enter(st, nc.sbuf_tensor(px + "wr_sb", [128, 256], BF16))
        Asb = _enter(st, nc.sbuf_tensor(px + "Asb", [128, 32 * 128], BF16))
        Bsb = _enter(st, nc.sbuf_tensor(px + "Bsb", [128, 4096], BF16))
        bb_row = _enter(st, nc.sbuf_tensor(px + "bbrow", [1, 4096], BF16))
        logitT_sb = _enter(st, nc.sbuf_tensor(px + "logitT", [8, T], F32))
        logit_sb = _enter(st, nc.sbuf_tensor(px + "logit", [128, 64], F32))
        e_sb = _enter(st, nc.sbuf_tensor(px + "e_sb", [128, 64], F32))
        w_sb = _enter(st, nc.sbuf_tensor(px + "w_sb", [128, 64], F32))
        wT_sb = _enter(st, nc.sbuf_tensor(px + "wT_sb", [8, T], F32))
        wt_rep = _enter(st, nc.sbuf_tensor(px + "wt_rep", [128, T], F32))
        htil = _enter(st, nc.sbuf_tensor(px + "htil", [128, T], BF16))
        col = _enter(st, nc.sbuf_tensor(px + "colG", [128, 1], F32))
        col2 = _enter(st, nc.sbuf_tensor(px + "colG2", [128, 1], F32))
        Ssel = _enter(st, nc.sbuf_tensor(px + "Ssel", [8, 128], F32))
        ones8 = _enter(st, nc.sbuf_tensor(px + "ones8", [8, 128], F32))
        xg = _enter(st, nc.sbuf_tensor(px + "xg", [128, 4 * 512], F32))

        sch.on("gpsimd", lambda E: E.memset(ones8[:, :], 1.0))
        ssel_tok = sch.on("gpsimd", lambda E: E.affine_select(
            Ssel[:, :], ones8[:, :], pattern=[[1, 8], [0, 16]], compare_op=OP.is_equal,
            fill=0.0, base=0, channel_multiplier=-1), inc=s_const)

        s_wload = sch.sem(px + "wload")
        nwl = 0
        sch.on("gpsimd", lambda E: E.dma_start(out=wr_sb[:, :], in_=wr_t[:, :], max_dma_last_dim=8192),
               inc=s_wload, inck=16, dma=True); nwl += 1
        sch.on("gpsimd", lambda E: E.dma_start(
            out=Asb[:, :].rearrange("p (j r) -> p j r", r=128),
            in_=AallT[:, :].rearrange("(j p) r -> p j r", p=128), max_dma_last_dim=8192),
            inc=s_wload, inck=16, dma=True); nwl += 1
        sch.on("gpsimd", lambda E: E.dma_start(out=Bsb[:, :], in_=BallT[:, :], max_dma_last_dim=8192),
               inc=s_wload, inck=16, dma=True); nwl += 1
        sch.on("gpsimd", lambda E: E.dma_start(out=bb_row[:, :], in_=bb.rearrange("(o n) -> o n", o=1),
                                               max_dma_last_dim=8192),
               inc=s_wload, inck=16, dma=True); nwl += 1
        wl_tok = (s_wload, 16 * nwl)

        s_mm = sch.sem(px + "mm")
        mm_tok = None
        for j in range(ND):
            for half in range(2):
                mm_tok = sch.on("tensor", lambda E, j=j, half=half: E.matmul(
                    psA[0:8, half * 512:(half + 1) * 512], wr_sb[:, j * 8:(j + 1) * 8],
                    xnT[:, j * T + half * 512: j * T + (half + 1) * 512],
                    start=(j == 0), stop=(j == ND - 1)),
                    wait=([wl_tok, xnT_tok] if (j == 0 and half == 0) else []),
                    inc=(s_mm if (j == ND - 1 and half == 1) else None))
        lg_tok = sch.on("scalar", lambda E: E.copy(logitT_sb[:, :], psA[0:8, :]), wait=[mm_tok])
        hm_tok = None
        for j in range(ND):
            for half in range(2):
                hm_tok = sch.on("tensor", lambda E, j=j, half=half: E.matmul(
                    psC[:, half * 512:(half + 1) * 512], Asb[:, j * 128:(j + 1) * 128],
                    xnT[:, j * T + half * 512: j * T + (half + 1) * 512],
                    start=(j == 0), stop=(j == ND - 1)),
                    wait=([wl_tok, xnT_tok] if (j == 0 and half == 0) else []),
                    inc=(s_mm if (j == ND - 1 and half == 1) else None))
        tp_tok = None
        for tc in range(8):
            tp_tok = sch.on("tensor", lambda E, tc=tc: E.transpose(
                psB[:, tc * 8:(tc + 1) * 8], logitT_sb[:, tc * 128:(tc + 1) * 128], ident_f32[0:8, 0:8]),
                wait=([lg_tok] if tc == 0 else []), inc=(s_mm if tc == 7 else None))
        lt_tok = sch.on("scalar", lambda E: E.copy(logit_sb[:, :], psB[:, 0:64]), wait=[tp_tok])
        wlast = None
        for tc in range(8):
            lv = logit_sb[:, tc * 8:(tc + 1) * 8]
            ev_ = e_sb[:, tc * 8:(tc + 1) * 8]
            wv_ = w_sb[:, tc * 8:(tc + 1) * 8]
            r1 = sch.on("vector", lambda E, lv=lv: E.tensor_reduce(col[:, :], lv, AX.X, OP.max, negate=True),
                        wait=[lt_tok])
            ex = sch.on("scalar", lambda E, lv=lv, ev_=ev_: E.activation(ev_, lv, AF.Exp, bias=col[:, 0:1]),
                        wait=[r1])
            m1 = sch.on("vector", lambda E, ev_=ev_: E.tensor_reduce(col[:, :], ev_, AX.X, OP.max), wait=[ex])
            e1 = sch.on("vector", lambda E, ev_=ev_, wv_=wv_: E.tensor_scalar(wv_, ev_, col[:, 0:1], None, OP.is_equal))
            e2 = sch.on("vector", lambda E, ev_=ev_, wv_=wv_: E.scalar_tensor_tensor(wv_, wv_, -2.0, ev_, OP.mult, OP.add))
            m2 = sch.on("vector", lambda E, wv_=wv_: E.tensor_reduce(col[:, :], wv_, AX.X, OP.max))
            ge = sch.on("vector", lambda E, ev_=ev_, wv_=wv_: E.tensor_scalar(wv_, ev_, col[:, 0:1], None, OP.is_ge))
            pw = sch.on("vector", lambda E, ev_=ev_, wv_=wv_: E.tensor_tensor(wv_, wv_, ev_, OP.mult))
            sm = sch.on("vector", lambda E, wv_=wv_: E.tensor_reduce(col[:, :], wv_, AX.X, OP.add))
            rc = sch.on("vector", lambda E: E.reciprocal(col2[:, :], col[:, :]))
            wlast = sch.on("vector", lambda E, wv_=wv_: E.tensor_scalar(wv_, wv_, col2[:, 0:1], None, OP.mult))
        wt_tok = None
        for tc in range(8):
            wt_tok = sch.on("tensor", lambda E, tc=tc: E.transpose(
                psB[0:8, tc * 128:(tc + 1) * 128], w_sb[:, tc * 8:(tc + 1) * 8], ident_f32[:, :]),
                wait=([wlast, lt_tok] if tc == 0 else []), inc=(s_mm if tc == 7 else None))
        wtc_tok = sch.on("scalar", lambda E: E.copy(wT_sb[:, :], psB[0:8, :]), wait=[wt_tok])
        wtm_tok = None
        for half in range(2):
            wtm_tok = sch.on("tensor", lambda E, half=half: E.matmul(
                psB[:, half * 512:(half + 1) * 512], Ssel[:, :], wT_sb[:, half * 512:(half + 1) * 512],
                start=True, stop=True),
                wait=([wtc_tok, ssel_tok] if half == 0 else []), inc=(s_mm if half == 1 else None))
        wrep_tok = sch.on("scalar", lambda E: E.copy(wt_rep[:, :], psB[:, :]), wait=[wtm_tok])
        ht_tok = sch.on("vector", lambda E: E.scalar_tensor_tensor(
            htil[:, :], psC[:, :], 2.0, wt_rep[:, :], OP.mult, OP.mult), wait=[wrep_tok, hm_tok])

        wbuf = [_enter(st, nc.sbuf_tensor(px + f"wbG{i}", [128, 8192], BF16)) for i in range(2)]
        s_wb = [sch.sem(px + "wb0"), sch.sem(px + "wb1")]
        s_xg = sch.sem(px + "xg")
        s_stG = sch.sem(px + "stG")
        n_wb = [0, 0]
        n_xg = [0]
        n_st = [0]
        wb_slot_tok = [None, None]
        bank_free = {0: ht_tok, 1: ht_tok, 2: ht_tok, 3: ht_tok}
        bank = lambda q: (psA if q < 2 else psB)[:, (q % 2) * 512:(q % 2 + 1) * 512]
        xg_tok_v = None
        st_tok = None
        for oc in range(8):
            for bslot in range(2):
                w = [wb_slot_tok[bslot]] if wb_slot_tok[bslot] else []
                n_wb[bslot] += 1
                sch.on("gpsimd", lambda E, oc=oc, bslot=bslot: E.dma_start(
                    out=wbuf[bslot][:, :].rearrange("p (j c) -> p j c", c=512),
                    in_=WbT[bslot * 2048:(bslot + 1) * 2048, oc * 512:(oc + 1) * 512].rearrange("(j p) c -> p j c", p=128),
                    max_dma_last_dim=8192), wait=w, inc=s_wb[bslot], inck=16, dma=True)
            wtoks = [(s_wb[0], 16 * n_wb[0]), (s_wb[1], 16 * n_wb[1])]
            for wv in range(2):
                tcs = [wv * 4 + q for q in range(4)]
                w = [st_tok] if st_tok else []
                n_xg[0] += 1
                sch.on("sync", lambda E, oc=oc, wv=wv: E.dma_start(
                    out=xg[:, :].rearrange("p (n c) -> p n c", c=512),
                    in_=x3[wv * 512:(wv + 1) * 512, oc * 512:(oc + 1) * 512].rearrange("(n p) c -> p n c", p=128)),
                    wait=w, inc=s_xg, inck=16)
                xgtok = (s_xg, 16 * n_xg[0])
                mmlast = None
                for j in range(ND):
                    bslot = j // 16
                    wtok = wtoks[bslot]
                    for q in range(4):
                        tc = tcs[q]
                        w2 = []
                        if q == 0:
                            w2 = [wtok]
                            if j == 0:
                                w2 += [t for t in (bank_free[qq] for qq in range(4)) if t]
                        mmlast = sch.on("tensor", lambda E, j=j, q=q, tc=tc, bslot=bslot: E.matmul(
                            bank(q), xnT[:, j * T + tc * 128: j * T + (tc + 1) * 128],
                            wbuf[bslot][:, (j % 16) * 512:(j % 16 + 1) * 512], start=(j == 0), stop=False),
                            wait=(w2 if q == 0 else []), inc=(s_mm if q == 3 else None))
                    if j == ND - 1:
                        wb_slot_tok[0] = mmlast
                        wb_slot_tok[1] = mmlast
                for q in range(4):
                    tc = tcs[q]
                    mmlast = sch.on("tensor", lambda E, q=q, tc=tc, oc=oc: E.matmul(
                        bank(q), htil[:, tc * 128:(tc + 1) * 128], Bsb[:, oc * 512:(oc + 1) * 512],
                        start=False, stop=False),
                        wait=([ht_tok] if q == 0 else []), inc=None)
                for q in range(4):
                    mmlast = sch.on("tensor", lambda E, q=q, oc=oc: E.matmul(
                        bank(q), ones_bf[0:1, :], bb_row[:, oc * 512:(oc + 1) * 512],
                        start=False, stop=True),
                        wait=([wl_tok] if q == 0 else []), inc=(s_mm if q == 3 else None))
                mmdone = (s_mm, s_mm.n)
                ev_last = None
                for q in range(4):
                    ev_last = sch.on("vector", lambda E, q=q: E.tensor_tensor(
                        xg[:, q * 512:(q + 1) * 512], bank(q), xg[:, q * 512:(q + 1) * 512], OP.add),
                        wait=([mmdone, xgtok] if q == 0 else []))
                    bank_free[q] = ev_last
                    n_st[0] += 1
                    sch.on("sync", lambda E, q=q, oc=oc, wv=wv: E.dma_start(
                        out=x4[wv * 512 + q * 128: wv * 512 + (q + 1) * 128, oc * 512:(oc + 1) * 512],
                        in_=xg[:, q * 512:(q + 1) * 512]),
                        wait=[ev_last], inc=s_stG, inck=16)
                st_tok = (s_stG, 16 * n_st[0])
        sch.on("sync", lambda E: E.nop(), wait=[(s_stG, 16 * n_st[0])])


def build_G():
    nc = bass.Bass("TRN2", target_bir_lowering=False, debug=False)
    x3 = nc.dram_tensor("x3", [T, D], F32, kind="ExternalInput").ap()
    WbT = nc.dram_tensor("WbT", [D, 4096], F32, kind="ExternalInput").ap()
    bb = nc.dram_tensor("bb", [4096], F32, kind="ExternalInput").ap()
    wr_t = nc.dram_tensor("wr_t", [128, 256], F32, kind="ExternalInput").ap()
    AallT = nc.dram_tensor("AallT", [D, 128], F32, kind="ExternalInput").ap()
    BallT = nc.dram_tensor("BallT", [128, 4096], F32, kind="ExternalInput").ap()
    x4 = nc.dram_tensor("x4", [T, D], F32, kind="ExternalOutput").ap()
    sch = Sched()
    with contextlib.ExitStack() as st:
        C = make_consts(sch, nc, st)
        emit_G(sch, nc, st, "g_", C, x3, WbT, bb, wr_t, AallT, BallT, x4)
        sch.run(nc)
    return nc


RG_QUAD = [[0, 1, 2, 3], [4, 5, 6, 7]]
RG_ALL = [[0, 1, 2, 3, 4, 5, 6, 7]]


def _ag(sch, name, in_ap, out_ap, groups):
    s_cc = sch.sem(name)
    sch.on("gpsimd", lambda E: E.collective_compute(
        "AllGather", mybir.AluOpType.bypass, replica_groups=groups,
        ins=[in_ap.opt()], outs=[out_ap.opt()]),
        inc=s_cc, inck=1, dma=True)
    return (s_cc, s_cc.n)


def build_fused():
    nc = bass.Bass("TRN2", target_bir_lowering=False, debug=False, monotonic_sem_count=8)
    I = 11008
    ext = lambda n, s, d: nc.dram_tensor(n, s, d, kind="ExternalInput").ap()
    itn = lambda n, s, d: nc.dram_tensor(n, s, d).ap()
    x_in = ext("x", [T, D], F32)
    wqT0 = ext("wqT0", [D, 512], F32)
    wkT0 = ext("wkT0", [D, 512], F32)
    wvT0 = ext("wvT0", [D, 512], F32)
    bq0 = ext("bq0", [512], F32)
    bk0 = ext("bk0", [512], F32)
    bv0 = ext("bv0", [512], F32)
    woT0 = ext("woT0", [512, D], F32)
    bo0 = ext("bo0", [D], F32)
    gq = ext("gq", [512, I], I32)
    gs = ext("gs", [4, I], F32)
    gz = ext("gz", [4, I], F32)
    uq = ext("uq", [512, I], I32)
    us = ext("us", [4, I], F32)
    uz = ext("uz", [4, I], F32)
    dq = ext("dq", [1408, D], I32)
    ds = ext("ds", [11, D], F32)
    dz = ext("dz", [11, D], F32)
    Wq1 = ext("Wq1", [D, 4096], F32)
    Wk1 = ext("Wk1", [D, 1024], F32)
    Wv1 = ext("Wv1", [D, 1024], F32)
    bq1 = ext("bq1", [4096], F32)
    bk1 = ext("bk1", [1024], F32)
    bv1 = ext("bv1", [1024], F32)
    posrow = ext("posrow", [1, T], F32)
    Wo1 = ext("Wo1", [4096, D], F32)
    bo1 = ext("bo1", [D], F32)
    visrow = ext("visrow", [1, 32], F32)
    WbT = ext("WbT", [D, 4096], F32)
    bb = ext("bb", [4096], F32)
    wr_t = ext("wr_t", [128, 256], F32)
    AallT = ext("AallT", [D, 128], F32)
    BallT = ext("BallT", [128, 4096], F32)
    out = nc.dram_tensor("out", [T, D], F32, kind="ExternalOutput").ap()

    q0_d = itn("q0_d", [512, T], BF16)
    k0_in = itn("k0_in", [512, T], BF16)
    v0_in = itn("v0_in", [T, 512], BF16)
    gk0 = itn("gk0", [2048, T], BF16)
    gv0 = itn("gv0", [4 * T, 512], BF16)
    x1_d = itn("x1_d", [T, D], F32)
    wgt_sl = itn("wgt_sl", [I, 512], BF16)
    wut_sl = itn("wut_sl", [I, 512], BF16)
    wd_sl = itn("wd_sl", [1408, D], BF16)
    gwg = nc.dram_tensor("gwg", [8 * I, 512], BF16, addr_space="Shared").ap()
    gwu = nc.dram_tensor("gwu", [8 * I, 512], BF16, addr_space="Shared").ap()
    gwd = nc.dram_tensor("gwd", [8 * 1408, D], BF16, addr_space="Shared").ap()
    x2_d = itn("x2_d", [T, D], F32)
    q1_d = itn("q1_d", [4096, T], BF16)
    k1_in = itn("k1_in", [1024, T], BF16)
    v1_in = itn("v1_in", [T, 1024], BF16)
    gk1 = itn("gk1", [4096, T], BF16)
    gv1 = itn("gv1", [4 * T, 1024], BF16)
    x3_d = itn("x3_d", [T, D], F32)

    def mono(i):
        s = Sem(f"mono{i}")
        s.h = nc.monotonic_semaphore(i).sem()
        s.n = 0
        return s

    monos = [mono(i) for i in range(7)]

    def issue_ag(sch, idx, in_ap, out_ap, groups, wait, eng="gpsimd"):
        m = monos[idx]
        sch.on(eng, lambda E: E.collective_compute(
            "AllGather", mybir.AluOpType.bypass, replica_groups=groups,
            ins=[in_ap.opt()], outs=[out_ap.opt()]),
            wait=wait, inc=m, inck=1, dma=True)
        return (m, m.n)

    def endphase():
        nc.reset(previous_local_dram_base=nc.local_dram_base,
                 previous_shared_dram_base=nc.shared_dram_base)

    sch = Sched()
    with contextlib.ExitStack() as st:
        Cc = make_consts(sch, nc, st)
        a_ps, a_guard = emit_A(sch, nc, st, "a_", Cc, x_in, wqT0, wkT0, wvT0, bq0, bk0, bv0, q0_d, k0_in, v0_in)
        atok = sch.lastany["sync"]
        t_k0 = issue_ag(sch, 0, k0_in, gk0, RG_QUAD, [atok])
        t_v0 = issue_ag(sch, 1, v0_in, gv0, RG_QUAD, [atok])
        ctok = emit_C(sch, nc, st, "c_", Cc, gq, gs, gz, uq, us, uz, dq, ds, dz,
                      wgt_sl.rearrange("(i p) f -> i p f", p=128),
                      wut_sl.rearrange("(i p) f -> i p f", p=128), wd_sl, tiled=True,
                      ps_ext=a_ps, ps_guard=a_guard)
        t_wg = issue_ag(sch, 2, wgt_sl, gwg, RG_ALL, [ctok])
        t_wu = issue_ag(sch, 3, wut_sl, gwu, RG_ALL, [ctok])
        t_wd = issue_ag(sch, 4, wd_sl, gwd, RG_ALL, [ctok])
        sch.run(nc)
    endphase()

    sch = Sched()
    with contextlib.ExitStack() as st:
        Cc = make_consts(sch, nc, st)
        kv_tok = [t_k0, t_v0]
        emit_B(sch, nc, st, "b_", Cc, q0_d, gk0, gv0, x_in, woT0, bo0, x1_d,
               gathered=True, kv_tok=[t_k0, t_v0])
        sch.run(nc)
    endphase()

    sch = Sched()
    with contextlib.ExitStack() as st:
        Cc = make_consts(sch, nc, st)
        emit_D(sch, nc, st, "d_", Cc, x1_d,
               gwg.rearrange("(ri p) f -> ri p f", p=128),
               gwu.rearrange("(ri p) f -> ri p f", p=128), gwd, x2_d,
               gathered=True, w_tok=[t_wg, t_wu, t_wd])
        sch.run(nc)
    endphase()

    sch = Sched()
    with contextlib.ExitStack() as st:
        Cc = make_consts(sch, nc, st)
        ag_out = {}

        def e_ag_cb(kv_toks):
            ag_out['k'] = issue_ag(sch, 5, k1_in, gk1, RG_QUAD, kv_toks, eng="gpsimd")
            ag_out['v'] = issue_ag(sch, 6, v1_in, gv1, RG_QUAD, kv_toks, eng="gpsimd")
        emit_E(sch, nc, st, "e_", Cc, x2_d, Wq1, Wk1, Wv1, bq1, bk1, bv1, posrow, q1_d, k1_in, v1_in,
               ag_cb=e_ag_cb)
        t_k1 = ag_out['k']
        t_v1 = ag_out['v']
        sch.run(nc)
    endphase()

    sch = Sched()
    with contextlib.ExitStack() as st:
        Cc = make_consts(sch, nc, st)
        emit_F(sch, nc, st, "f_", Cc, q1_d, gk1, gv1, x2_d, Wo1, bo1, visrow, x3_d,
               gathered=True, kv_tok=[t_k1, t_v1])
        sch.run(nc)
    endphase()

    sch = Sched()
    with contextlib.ExitStack() as st:
        Cc = make_consts(sch, nc, st)
        emit_G(sch, nc, st, "g_", Cc, x3_d, WbT, bb, wr_t, AallT, BallT, out)
        sch.run(nc)
    return nc


def _host_prep(inputs):
    g = {k: np.asarray(v) for k, v in inputs.items()}
    x = np.ascontiguousarray(g['L0_attn_x'].reshape(2, 4096, 4096).astype(np.float32, copy=False))
    xs = [np.ascontiguousarray(x[c // 4, (c % 4) * T:(c % 4 + 1) * T]) for c in range(NCORES)]
    Wq0, Wk0, bq0, bk0 = _deinterleave_l0(g['L0_attn_Wqk_w'], g['L0_attn_Wqk_b'])
    prep = dict(
        wqT0=np.ascontiguousarray(Wq0.T), wkT0=np.ascontiguousarray(Wk0.T),
        wvT0=np.ascontiguousarray(g['L0_attn_Wv_w'].T),
        bq0=bq0, bk0=bk0, bv0=g['L0_attn_Wv_b'],
        woT0=np.ascontiguousarray(g['L0_attn_Wo_w'].T), bo0=g['L0_attn_Wo_b'],
        Wq1=g['L1_attn_Wq'], Wk1=g['L1_attn_Wk'], Wv1=g['L1_attn_Wv'],
        bq1=g['L1_attn_bq'], bk1=g['L1_attn_bk'], bv1=g['L1_attn_bv'],
        Wo1=g['L1_attn_Wo'], bo1=g['L1_attn_bo'],
        WbT=np.ascontiguousarray(g['L1_ffn_W_base'].T), bb=g['L1_ffn_b_base'],
        wr_t=np.ascontiguousarray(g['L1_ffn_W_router'].T.reshape(32, 128, 8).transpose(1, 0, 2).reshape(128, 256)),
        AallT=np.ascontiguousarray(g['L1_ffn_lora_A'].reshape(128, 4096).T),
        BallT=np.ascontiguousarray(g['L1_ffn_lora_B'].transpose(0, 2, 1).reshape(128, 4096)),
    )
    percore = []
    for c in range(NCORES):
        g0 = c * 11
        ngr = min(11, 86 - g0)
        dqp = np.zeros((1408, 4096), np.int32)
        dsp = np.zeros((11, 4096), np.float32)
        dzp = np.zeros((11, 4096), np.float32)
        dqp[:ngr * 128] = g['L0_ffn_down_q'][g0 * 128:(g0 + ngr) * 128]
        dsp[:ngr] = g['L0_ffn_down_s'][g0:g0 + ngr]
        dzp[:ngr] = g['L0_ffn_down_z'][g0:g0 + ngr]
        vis = np.zeros((1, 32), np.float32)
        rot = (c % 4) * 8
        for s in range(8, 32):
            kt = (rot + s) % 32
            vis[0, s] = 0.0 if kt < rot else np.float32(-3e38)
        percore.append(dict(
            x=xs[c],
            gq=np.ascontiguousarray(g['L0_ffn_gate_q'][c * 512:(c + 1) * 512]),
            gs=np.ascontiguousarray(g['L0_ffn_gate_s'][c * 4:(c + 1) * 4]),
            gz=np.ascontiguousarray(g['L0_ffn_gate_z'][c * 4:(c + 1) * 4]),
            uq=np.ascontiguousarray(g['L0_ffn_up_q'][c * 512:(c + 1) * 512]),
            us=np.ascontiguousarray(g['L0_ffn_up_s'][c * 4:(c + 1) * 4]),
            uz=np.ascontiguousarray(g['L0_ffn_up_z'][c * 4:(c + 1) * 4]),
            dq=dqp, ds=dsp, dz=dzp,
            posrow=np.arange((c % 4) * T, (c % 4 + 1) * T, dtype=np.float32)[None, :],
            visrow=vis,
        ))
    return g, prep, percore


def kernel(**inputs):
    g, prep, percore = _host_prep(inputs)
    nc = build_fused()
    maps = [dict(prep, **percore[c]) for c in range(NCORES)]
    res = run_bass_kernel_spmd(nc, maps, core_ids=list(range(NCORES)))
    outa = np.empty((2, 4096, 4096), np.float32)
    for c in range(NCORES):
        outa[c // 4, (c % 4) * T:(c % 4 + 1) * T] = res.results[c]['out']
    return outa


def _deinterleave_l0(Wqk, bqk):
    idx = np.arange(1024).reshape(8, 64, 2)
    qi = idx[..., 0].reshape(-1)
    ki = idx[..., 1].reshape(-1)
    return Wqk[qi], Wqk[ki], bqk[qi], bqk[ki]


def _pack_inmaps(builder, maps):
    nc = builder()
    return run_bass_kernel_spmd(nc, maps, core_ids=list(range(NCORES)))


def kernel_unfused(**inputs):
    g = {k: np.asarray(v) for k, v in inputs.items()}
    x = np.ascontiguousarray(g['L0_attn_x'].reshape(2, 4096, 4096).astype(np.float32, copy=False))
    xs = [np.ascontiguousarray(x[c // 4, (c % 4) * T:(c % 4 + 1) * T]) for c in range(NCORES)]

    Wq0, Wk0, bq0, bk0 = _deinterleave_l0(g['L0_attn_Wqk_w'], g['L0_attn_Wqk_b'])
    wqT = np.ascontiguousarray(Wq0.T)
    wkT = np.ascontiguousarray(Wk0.T)
    wvT = np.ascontiguousarray(g['L0_attn_Wv_w'].T)
    res = _pack_inmaps(build_A, [dict(x=xs[c], wqT=wqT, wkT=wkT, wvT=wvT,
                                      bq=bq0, bk=bk0, bv=g['L0_attn_Wv_b']) for c in range(NCORES)])
    qT = [res.results[c]['qT'] for c in range(NCORES)]
    kT = [res.results[c]['kT'] for c in range(NCORES)]
    vv = [res.results[c]['v'] for c in range(NCORES)]

    kTf = [np.ascontiguousarray(np.concatenate([kT[4 * b + i] for i in range(4)], axis=1)) for b in range(2)]
    vf = [np.ascontiguousarray(np.concatenate([vv[4 * b + i] for i in range(4)], axis=0)) for b in range(2)]
    woT = np.ascontiguousarray(g['L0_attn_Wo_w'].T)
    res = _pack_inmaps(build_B, [dict(qT=np.ascontiguousarray(qT[c]), kTf=kTf[c // 4], vf=vf[c // 4],
                                      x=xs[c], woT=woT, bo=g['L0_attn_Wo_b']) for c in range(NCORES)])
    x1 = [res.results[c]['x1'] for c in range(NCORES)]

    def down_slice(arrq, arrs, arrz, c):
        g0 = c * 11
        ngr = min(11, 86 - g0)
        dq = np.zeros((1408, 4096), np.int32)
        ds = np.zeros((11, 4096), np.float32)
        dz = np.zeros((11, 4096), np.float32)
        dq[:ngr * 128] = arrq[g0 * 128:(g0 + ngr) * 128]
        ds[:ngr] = arrs[g0:g0 + ngr]
        dz[:ngr] = arrz[g0:g0 + ngr]
        return dq, ds, dz
    maps = []
    for c in range(NCORES):
        dq, ds, dz = down_slice(g['L0_ffn_down_q'], g['L0_ffn_down_s'], g['L0_ffn_down_z'], c)
        maps.append(dict(
            gq=np.ascontiguousarray(g['L0_ffn_gate_q'][c * 512:(c + 1) * 512]),
            gs=np.ascontiguousarray(g['L0_ffn_gate_s'][c * 4:(c + 1) * 4]),
            gz=np.ascontiguousarray(g['L0_ffn_gate_z'][c * 4:(c + 1) * 4]),
            uq=np.ascontiguousarray(g['L0_ffn_up_q'][c * 512:(c + 1) * 512]),
            us=np.ascontiguousarray(g['L0_ffn_up_s'][c * 4:(c + 1) * 4]),
            uz=np.ascontiguousarray(g['L0_ffn_up_z'][c * 4:(c + 1) * 4]),
            dq=dq, ds=ds, dz=dz))
    res = _pack_inmaps(build_C, maps)
    Wg = np.concatenate([res.results[c]['wg'] for c in range(NCORES)], axis=0)
    Wu = np.concatenate([res.results[c]['wu'] for c in range(NCORES)], axis=0)
    Wd = np.concatenate([res.results[c]['wd'][:min(11, 86 - c * 11) * 128] for c in range(NCORES)], axis=0)

    def tile_w(W):
        W4 = W.reshape(32, 128, 86, 128)
        return np.ascontiguousarray(W4.transpose(2, 1, 0, 3).reshape(86, 128, 4096))
    wg_h = tile_w(Wg)
    wu_h = tile_w(Wu)
    Wd = np.ascontiguousarray(Wd)
    res = _pack_inmaps(build_D, [dict(x1=np.ascontiguousarray(x1[c]), wg_h=wg_h, wu_h=wu_h, wd=Wd)
                                 for c in range(NCORES)])
    x2 = [res.results[c]['x2'] for c in range(NCORES)]

    res = _pack_inmaps(build_E, [dict(
        x2=np.ascontiguousarray(x2[c]), Wq=g['L1_attn_Wq'], Wk=g['L1_attn_Wk'], Wv=g['L1_attn_Wv'],
        bq=g['L1_attn_bq'], bk=g['L1_attn_bk'], bv=g['L1_attn_bv'],
        posrow=np.arange((c % 4) * T, (c % 4 + 1) * T, dtype=np.float32)[None, :]) for c in range(NCORES)])
    q1T = [res.results[c]['q1T'] for c in range(NCORES)]
    k1T = [res.results[c]['k1T'] for c in range(NCORES)]
    v1 = [res.results[c]['v1'] for c in range(NCORES)]

    kTfull = [np.concatenate([k1T[4 * b + i] for i in range(4)], axis=1) for b in range(2)]
    vfull = [np.concatenate([v1[4 * b + i] for i in range(4)], axis=0) for b in range(2)]
    maps = []
    for c in range(NCORES):
        b = c // 4
        rot = (c % 4) * 8
        perm = [((rot + s) % 32) for s in range(32)]
        kTr = np.ascontiguousarray(np.concatenate([kTfull[b][:, kt * 128:(kt + 1) * 128] for kt in perm], axis=1))
        v1gr = np.ascontiguousarray(np.stack(
            [np.concatenate([vfull[b][kt * 128:(kt + 1) * 128, gg * 128:(gg + 1) * 128] for kt in perm], axis=0)
             for gg in range(8)]))
        vis = np.zeros((1, 32), np.float32)
        for s in range(8, 32):
            kt = (rot + s) % 32
            vis[0, s] = 0.0 if kt < rot else np.float32(-3e38)
        maps.append(dict(q1T=np.ascontiguousarray(q1T[c]), kTr=kTr, v1g=v1gr,
                         x2=np.ascontiguousarray(x2[c]), Wo=g['L1_attn_Wo'], bo=g['L1_attn_bo'], visrow=vis))
    res = _pack_inmaps(build_F, maps)
    x3 = [res.results[c]['x3'] for c in range(NCORES)]

    WbT = np.ascontiguousarray(g['L1_ffn_W_base'].T)
    wr_t = np.ascontiguousarray(g['L1_ffn_W_router'].T.reshape(32, 128, 8).transpose(1, 0, 2).reshape(128, 256))
    AallT = np.ascontiguousarray(g['L1_ffn_lora_A'].reshape(128, 4096).T)
    BallT = np.ascontiguousarray(g['L1_ffn_lora_B'].transpose(0, 2, 1).reshape(128, 4096))
    res = _pack_inmaps(build_G, [dict(x3=np.ascontiguousarray(x3[c]), WbT=WbT, bb=g['L1_ffn_b_base'],
                                      wr_t=wr_t, AallT=AallT, BallT=BallT) for c in range(NCORES)])
    x4 = [res.results[c]['x4'] for c in range(NCORES)]

    out = np.empty((2, 4096, 4096), np.float32)
    for c in range(NCORES):
        out[c // 4, (c % 4) * T:(c % 4 + 1) * T] = x4[c]
    return out


def run_A(inputs, x_full):
    Wq0, Wk0, bq0, bk0 = _deinterleave_l0(inputs["L0_attn_Wqk_w"], inputs["L0_attn_Wqk_b"])
    wqT = np.ascontiguousarray(Wq0.T)
    wkT = np.ascontiguousarray(Wk0.T)
    wvT = np.ascontiguousarray(inputs["L0_attn_Wv_w"].T)
    nc = build_A()
    in_maps = [dict(x=np.ascontiguousarray(x_full[c]), wqT=wqT, wkT=wkT, wvT=wvT,
                    bq=bq0, bk=bk0, bv=inputs["L0_attn_Wv_b"]) for c in range(NCORES)]
    res = run_bass_kernel_spmd(nc, in_maps, core_ids=list(range(NCORES)))
    qT = np.stack([res.results[c]["qT"] for c in range(NCORES)])
    kT = np.stack([res.results[c]["kT"] for c in range(NCORES)])
    v = np.stack([res.results[c]["v"] for c in range(NCORES)])
    return qT, kT, v



